# version 03
# baseline (speedup 1.0000x reference)
import numpy as np
import ml_dtypes
import concourse.bass as bass
import concourse.mybir as mybir
from concourse.bass_utils import run_bass_kernel_spmd

F32 = mybir.dt.float32
BF16 = mybir.dt.bfloat16
AF = mybir.ActivationFunctionType
ALU = mybir.AluOpType
AX = mybir.AxisListType

P = 128
D = 1024
S = 4096
T = 512
NS = T // P
NT = S // T
INNER = 4096
NH = 16
HD = 64
CTX = 1024
POOL = 256
C_RNE = float(np.float32(3 * 2.0 ** 22))
EPS = 1e-5

SBUF_BASE = 16512
OFF_WS = SBUF_BASE
OFF_GU = SBUF_BASE + 65536
OFF_Q1 = SBUF_BASE + 131072
OFF_XQT = SBUF_BASE + 147456
OFF_XQS = SBUF_BASE + 155648
OFF_X0 = SBUF_BASE + 157696
OFF_X1 = SBUF_BASE + 174080
OFF_KCT = SBUF_BASE + 190464
OFF_VC = SBUF_BASE + 194560
OFF_STAT = SBUF_BASE + 198656
OFF_CONST = SBUF_BASE + 199680


def carve(nc, name, shape, dtype, off):
    h = nc.virtual_sbuf_tensor(name, shape, dtype)
    nc._tensor(h.name, list(h.shape), h.dtype, type="SB")
    mloc = nc.lookup_mloc(h)
    mloc.allocated = not nc.target_bir_lowering
    mloc.addr = off
    return h


class Walk:

    def __init__(self, nc, ek, eng, sems):
        self.nc = nc
        self.ek = ek
        self.eng = eng
        self.sems = sems
        self.cnt = {k: 0 for k in sems}

    def inc(self, ch, instr, n=1):
        self.cnt[ch] += n
        if instr is not None:
            instr.then_inc(self.sems[ch], n)

    def dma(self, ek, ch, out, in_, **kw):
        self.cnt[ch] += 16
        if self.ek == ek:
            self.eng.dma_start(out=out, in_=in_, **kw).then_inc(self.sems[ch], 16)

    def wait(self, ek, ch, val=None):
        if self.ek == ek:
            self.eng.wait_ge(self.sems[ch], self.cnt[ch] if val is None else val)

    def on(self, ek):
        return self.ek == ek

    def drain(self, ek):
        if self.ek == ek:
            self.eng.drain()


def build_program(stop=None):
    nc = bass.Bass("TRN2", target_bir_lowering=False)

    x_in = nc.dram_tensor("x", [S, D], F32, kind="ExternalInput").ap()
    wq_in = nc.dram_tensor("wq", [D, D], F32, kind="ExternalInput").ap()
    wk_in = nc.dram_tensor("wk", [D, D], F32, kind="ExternalInput").ap()
    wv_in = nc.dram_tensor("wv", [D, D], F32, kind="ExternalInput").ap()
    wo_in = nc.dram_tensor("wo", [D, D], F32, kind="ExternalInput").ap()
    w1_in = nc.dram_tensor("w1", [D, 512], F32, kind="ExternalInput").ap()
    b1_in = nc.dram_tensor("b1", [512], F32, kind="ExternalInput").ap()
    w2_in = nc.dram_tensor("w2", [512, D], F32, kind="ExternalInput").ap()
    b2_in = nc.dram_tensor("b2", [D], F32, kind="ExternalInput").ap()
    wg_in = nc.dram_tensor("wg", [INNER, D], F32, kind="ExternalInput").ap()
    wu_in = nc.dram_tensor("wu", [INNER, D], F32, kind="ExternalInput").ap()
    wd_in = nc.dram_tensor("wd", [D, INNER], F32, kind="ExternalInput").ap()
    out = nc.dram_tensor("out", [S, D], F32, kind="ExternalOutput").ap()

    s_wgT = nc.dram_tensor("s_wgT", [D, INNER], BF16, kind="Internal").ap()
    s_wuT = nc.dram_tensor("s_wuT", [D, INNER], BF16, kind="Internal").ap()
    s_wdT = nc.dram_tensor("s_wdT", [INNER, D], BF16, kind="Internal").ap()
    s_att = {}
    for k, rows, cols in [("wq", D, D), ("wk", D, D), ("wv", D, D), ("wo", D, D),
                          ("w1", D, 512), ("w2", 512, D)]:
        s_att[k] = nc.dram_tensor("s_" + k, [rows, cols], BF16, kind="Internal").ap()

    identb_t = nc.inline_tensor(np.eye(P, dtype=ml_dtypes.bfloat16), name="c_identb")
    identf_t = nc.inline_tensor(np.eye(P, dtype=np.float32), name="c_identf")
    maskc = np.where(np.arange(P)[None, :] <= np.arange(P)[:, None],
                     np.float32(0.0), np.float32(-1e30)).astype(np.float32)
    mask_t = nc.inline_tensor(maskc, name="c_mask")
    ones_t = nc.inline_tensor(np.ones((1, P), dtype=np.float32), name="c_ones")
    cvals = np.broadcast_to(np.array([[C_RNE, 0.125]], dtype=np.float32), (P, 2)).copy()
    cval_t = nc.inline_tensor(cvals, name="c_cvals")

    ws = carve(nc, "ws", [P, 4, 8, 1024], BF16, OFF_WS)
    gu = carve(nc, "gu", [P, NS, INNER], F32, OFF_GU)
    guq = [carve(nc, f"guq{s}", [P, INNER], BF16, OFF_GU + s * 16384) for s in range(NS)]
    q1 = carve(nc, "q1", [P, 4096], F32, OFF_Q1)
    gqT = [carve(nc, f"gqT{i}", [P, 32, P], BF16, OFF_Q1 + i * 8192) for i in range(2)]
    xqT = carve(nc, "xqT", [P, 8, T], BF16, OFF_XQT)
    xqs = carve(nc, "xqs", [P, D], BF16, OFF_XQS)
    xsb = [carve(nc, "x0", [P, NS, D], F32, OFF_X0),
           carve(nc, "x1", [P, NS, D], F32, OFF_X1)]
    kct = carve(nc, "kct", [P, 8, POOL], BF16, OFF_KCT)
    vc = carve(nc, "vc", [P, 2, D], BF16, OFF_VC)
    st = carve(nc, "st", [P, 256], F32, OFF_STAT)
    identb = carve(nc, "identb", [P, P], BF16, OFF_CONST)
    identf = carve(nc, "identf", [P, P], F32, OFF_CONST + 256)
    maskb = carve(nc, "maskb", [P, P], F32, OFF_CONST + 768)
    onesb = carve(nc, "onesb", [1, P], F32, OFF_CONST + 1280)
    b1b = carve(nc, "b1b", [P, 4], F32, OFF_CONST + 1792)
    b2b = carve(nc, "b2b", [P, 8], F32, OFF_CONST + 1808)
    cvalb = carve(nc, "cvalb", [P, 2], F32, OFF_CONST + 1840)
    lnT2 = carve(nc, "lnT2", [P, 8, T], BF16, OFF_GU + 0)
    qt2 = carve(nc, "qt2", [P, 8, T], BF16, OFF_GU + 8192)
    ot2 = carve(nc, "ot2", [P, 8, T], BF16, OFF_GU + 16384)
    pr2 = carve(nc, "pr2", [P, NH, POOL], BF16, OFF_GU + 24576)
    prT2 = carve(nc, "prT2", [P, 2, NH, P], BF16, OFF_GU + 32768)
    lns = carve(nc, "lns", [P, D], BF16, OFF_GU + 40960)
    lnT1 = carve(nc, "lnT1", [P, 8, CTX], BF16, OFF_GU + 0)
    qt1 = carve(nc, "qt1", [P, 8, CTX], BF16, OFF_GU + 16384)
    kt1 = carve(nc, "kt1", [P, 8, CTX], BF16, OFF_GU + 32768)
    v1 = carve(nc, "v1", [P, 8, D], BF16, OFF_GU + 49152)
    ot1 = carve(nc, "ot1", [P, 8, CTX], BF16, OFF_Q1 + 0)
    relu1 = carve(nc, "relu1", [P, 4, CTX], BF16, OFF_XQT)
    pool1 = carve(nc, "pool1", [P, 4, POOL], BF16, OFF_XQS)
    compT = carve(nc, "compT", [P, 8, POOL], BF16, OFF_XQT)
    pr1 = [carve(nc, f"pr1_{i}", [P, CTX], BF16, OFF_XQT + i * 2048) for i in range(2)]
    prT1 = [carve(nc, f"prT1_{i}", [P, 8, P], BF16, OFF_XQT + 4096 + i * 2048)
            for i in range(2)]
    wfg = carve(nc, "wfg", [P, 32, D], F32, OFF_WS)
    wfd = carve(nc, "wfd", [P, 8, INNER], F32, OFF_WS)
    wqj = carve(nc, "wqj", [P, 4096], BF16, OFF_KCT)
    wcol = carve(nc, "wcol", [P, 8, 1024], BF16, OFF_Q1 + 4096)
    wdcol = carve(nc, "wdcol", [P, 32, P], BF16, OFF_XQT)

    nc.const_aps.register(F32, C_RNE, cvalb[:, 0:1])
    nc.const_aps.register(F32, 0.125, cvalb[:, 1:2])

    COL = dict(sums=0, sumsq=4, mx=8, mn=12, mean=16, m2=20, var=24, rstd=28,
               amax=32, sx=36, isx=40, qsc=44, qb=48, gsc=52, usc=56,
               gamax=60, gsx=64, gisx=68, dsc=72, tmp=76,
               negmax=80, sume=96, rsum=112,
               wtmp=128, swrep=224, mwg=225, mwu=226, mwd=227)

    def stc(name, s=0):
        c = COL[name] + s
        return st[:, c:c + 1]

    def stv(name, w, s=0):
        c = COL[name] + s
        return st[:, c:c + w]

    from contextlib import ExitStack
    ctx = ExitStack()
    sems = {k: ctx.enter_context(nc.semaphore(k)) for k in ['SD', 'GD', 'PE', 'V', 'A']}
    psa = ctx.enter_context(nc.psum_tensor("psa", [P, 2048], F32))
    psb = ctx.enter_context(nc.psum_tensor("psb", [P, 1024], F32))
    psc = ctx.enter_context(nc.psum_tensor("psc", [P, 2048], BF16))
    block = ctx.enter_context(nc.Block())

    def program(p: Walk):
        slot_rel = {0: 0, 1: 0, 2: 0, 3: 0}

        def mm_accum(p_, out_ap, ops, inc=True):
            last = None
            if p_.on('tensor'):
                n = len(ops)
                for i, (l, r) in enumerate(ops):
                    last = p_.eng.matmul(out_ap, l, r, start=(i == 0), stop=(i == n - 1))
            if inc:
                p_.inc('PE', last)
            return last

        def transpose_batch(p_, pairs, inc=True):
            last = None
            if p_.on('tensor'):
                for o, i in pairs:
                    last = p_.eng.transpose(o, i, identb[:])
            if inc:
                p_.inc('PE', last)
            return last

        def pe_phase_barrier():
            p.wait('tensor', 'V')
            p.wait('tensor', 'A')

        p.dma('sync', 'SD', identb[:], identb_t.ap())
        p.dma('sync', 'SD', identf[:], identf_t.ap())
        p.dma('sync', 'SD', maskb[:], mask_t.ap())
        p.dma('sync', 'SD', onesb[:], ones_t.ap())
        p.dma('sync', 'SD', cvalb[:], cval_t.ap())
        p.dma('sync', 'SD', b1b[:], b1_in.rearrange("(c p) -> p c", p=P),
              allow_slow_non_contiguous=True)
        p.dma('sync', 'SD', b2b[:], b2_in.rearrange("(c p) -> p c", p=P),
              allow_slow_non_contiguous=True)
        p.dma('gpsimd', 'GD', xsb[0][:], x_in[0:T, :].rearrange("(s p) d -> p s d", p=P))
        p.dma('gpsimd', 'GD', xsb[1][:], x_in[T:2 * T, :].rearrange("(s p) d -> p s d", p=P))

        def quant_weight(w_src, n_rows, scr_T, mw_col):
            rl = D if n_rows == 32 else INNER
            wf = wfg if n_rows == 32 else wfd
            nelem = float(n_rows * P * rl)
            p.wait('sync', 'A')
            p.dma('sync', 'SD', wf[:], w_src.rearrange("(j p) i -> p j i", p=P))
            p.wait('scalar', 'SD')
            if p.on('scalar'):
                for j in range(n_rows):
                    ins = p.eng.activation(out=q1[:, 0:rl], in_=wf[:, j, :], func=AF.Abs,
                                           accum_out=stv('wtmp', 1, j))
                ins.then_inc(p.sems['A'], 1)
            p.cnt['A'] += 1
            p.wait('vector', 'A')
            if p.on('vector'):
                ins = p.eng.reduce_sum(out=stv('wtmp', 1, 64), in_=stv('wtmp', n_rows, 0),
                                       axis=AX.X)
                ins.then_inc(p.sems['V'], 1)
                p.eng.drain()
            p.cnt['V'] += 1
            p.wait('tensor', 'V')
            if p.on('tensor'):
                ins = p.eng.transpose(psa[0:1, 0:P], stv('wtmp', 1, 64), identf[:])
            p.inc('PE', ins if p.on('tensor') else None)
            p.wait('vector', 'PE')
            if p.on('vector'):
                e = p.eng
                e.tensor_copy(out=st[0:1, 0:P], in_=psa[0:1, 0:P])
                e.drain()
                c65 = st[0:1, COL['wtmp'] + 65:COL['wtmp'] + 66]
                c66 = st[0:1, COL['wtmp'] + 66:COL['wtmp'] + 67]
                c67 = st[0:1, COL['wtmp'] + 67:COL['wtmp'] + 68]
                e.reduce_sum(out=c65, in_=st[0:1, 0:P], axis=AX.X)
                e.drain()
                e.tensor_scalar(out=c65, in0=c65, scalar1=1.0 / nelem, scalar2=1e-5,
                                op0=ALU.mult, op1=ALU.max)
                e.drain()
                e.reciprocal(out=c66, in_=c65)
                e.drain()
                ins = e.reciprocal(out=c67, in_=c66)
                ins.then_inc(p.sems['V'], 1)
                e.drain()
            p.cnt['V'] += 1
            p.wait('tensor', 'V')
            if p.on('tensor'):
                ins = p.eng.matmul(psa[:, 256:258], onesb[0:1, 0:P],
                                   st[0:1, COL['wtmp'] + 66:COL['wtmp'] + 68],
                                   start=True, stop=True)
            p.inc('PE', ins if p.on('tensor') else None)
            p.wait('vector', 'PE')
            if p.on('vector'):
                p.eng.tensor_copy(out=stc('swrep'), in_=psa[:, 256:257])
                ins = p.eng.tensor_copy(out=st[:, mw_col:mw_col + 1], in_=psa[:, 257:258])
                p.eng.drain()
                ins.then_inc(p.sems['V'], 1)
            p.cnt['V'] += 1

            for j in range(n_rows):
                p.wait('scalar', 'V')
                if p.on('scalar'):
                    ins = p.eng.activation(out=q1[:, 0:rl], in_=wf[:, j, :],
                                           func=AF.Identity, scale=stc('swrep'),
                                           bias=C_RNE)
                    ins.then_inc(p.sems['A'], 1)
                p.cnt['A'] += 1
                p.wait('vector', 'A')
                p.wait('vector', 'PE')
                if n_rows == 32 and j % 8 == 0:
                    p.wait('vector', 'GD')
                if p.on('vector'):
                    p.eng.tensor_scalar(out=q1[:, 0:rl], in0=q1[:, 0:rl], scalar1=C_RNE,
                                        scalar2=1.0, op0=ALU.subtract, op1=ALU.min)
                    p.eng.drain()
                    ins = p.eng.tensor_scalar(out=wqj[:, 0:rl], in0=q1[:, 0:rl],
                                              scalar1=-1.0, scalar2=None, op0=ALU.max)
                    ins.then_inc(p.sems['V'], 1)
                p.cnt['V'] += 1
                ntp = rl // P
                p.wait('tensor', 'V')
                for b in range(ntp // 8):
                    ph = b % 2
                    p.wait('tensor', 'V')
                    pairs = [(psc[:, ph * 1024 + k * P:ph * 1024 + (k + 1) * P],
                              wqj[:, (b * 8 + k) * P:(b * 8 + k + 1) * P])
                             for k in range(8)]
                    transpose_batch(p, pairs)
                    p.wait('vector', 'PE')
                    if n_rows == 8 and b == 0:
                        p.wait('vector', 'GD')
                    if p.on('vector'):
                        src = psc[:, ph * 1024:(ph + 1) * 1024].rearrange(
                            "p (a b) -> p a b", b=P)
                        if n_rows == 32:
                            ins = p.eng.tensor_copy(
                                out=wcol[:, :, (j % 8) * P:(j % 8 + 1) * P], in_=src)
                        else:
                            ins = p.eng.tensor_copy(
                                out=wdcol[:, b * 8:(b + 1) * 8, :], in_=src)
                        ins.then_inc(p.sems['V'], 1)
                    p.cnt['V'] += 1
                if n_rows == 32 and j % 8 == 7:
                    jg = j // 8
                    p.wait('gpsimd', 'V')
                    p.dma('gpsimd', 'GD',
                          scr_T.rearrange("(ic p) o -> p ic o", p=P)[:, :, jg * 1024:(jg + 1) * 1024],
                          wcol[:])
                elif n_rows == 8:
                    p.wait('gpsimd', 'V')
                    p.dma('gpsimd', 'GD',
                          scr_T.rearrange("(ic p) o -> p ic o", p=P)[:, :, j * P:(j + 1) * P],
                          wdcol[:])

        quant_weight(wg_in, 32, s_wgT, COL['mwg'])
        if stop == 'dbgq':
            p.wait('sync', 'GD')
            p.dma('sync', 'SD', out[0:1024, 0:1024],
                  s_wgT.bitcast(F32)[:, 0:1024])
            p.wait('sync', 'V')
            p.wait('sync', 'A')
            p.dma('sync', 'SD', out[1024:1152, 0:256], st[:])
            p.wait('sync', 'SD')
            return
        quant_weight(wu_in, 32, s_wuT, COL['mwu'])
        quant_weight(wd_in, 8, s_wdT, COL['mwd'])

        p.wait('gpsimd', 'A')
        p.wait('gpsimd', 'V')
        p.wait('gpsimd', 'PE')
        ws_flat = ws.rearrange("p a b c -> p (a b) c")
        for key, src, rows, cols in [("wq", wq_in, D, D), ("wk", wk_in, D, D),
                                     ("wv", wv_in, D, D), ("wo", wo_in, D, D),
                                     ("w1", w1_in, D, 512), ("w2", w2_in, 512, D)]:
            p.wait('gpsimd', 'GD')
            p.dma('gpsimd', 'GD', ws_flat[:, 0:rows // P, 0:cols],
                  src.rearrange("(j p) o -> p j o", p=P))
            p.wait('gpsimd', 'GD')
            p.dma('gpsimd', 'GD', s_att[key].rearrange("(j p) o -> p j o", p=P),
                  ws_flat[:, 0:rows // P, 0:cols])
        cast_done_gd = p.cnt['GD']


        def ln_quant_stats(xb):
            p.wait('vector', 'GD')
            p.wait('scalar', 'GD')
            p.wait('scalar', 'PE')
            p.wait('scalar', 'V')
            p.drain('vector')
            if p.on('vector'):
                for s in range(NS):
                    p.eng.reduce_sum(out=stc('sums', s), in_=xb[:, s, :], axis=AX.X)
                    p.eng.reduce_max(out=stc('mx', s), in_=xb[:, s, :], axis=AX.X)
                    ins = p.eng.tensor_reduce(out=stc('mn', s), in_=xb[:, s, :],
                                              axis=AX.X, op=ALU.min)
                ins.then_inc(p.sems['V'], 1)
            p.cnt['V'] += 1
            if p.on('scalar'):
                for s in range(NS):
                    ins = p.eng.activation(out=q1[:, 0:D], in_=xb[:, s, :],
                                           func=AF.Square, accum_out=stc('sumsq', s))
                ins.then_inc(p.sems['A'], 1)
            p.cnt['A'] += 1
            p.wait('vector', 'A')
            if p.on('vector'):
                e = p.eng
                e.drain()
                e.tensor_scalar(out=stv('mean', NS), in0=stv('sums', NS),
                                scalar1=1.0 / D, scalar2=None, op0=ALU.mult)
                e.drain()
                e.tensor_mul(out=stv('m2', NS), in0=stv('mean', NS), in1=stv('mean', NS))
                e.drain()
                e.scalar_tensor_tensor(out=stv('var', NS), in0=stv('sumsq', NS),
                                       scalar=1.0 / D, in1=stv('m2', NS),
                                       op0=ALU.mult, op1=ALU.subtract)
                e.drain()
                e.tensor_scalar(out=stv('var', NS), in0=stv('var', NS), scalar1=EPS,
                                scalar2=None, op0=ALU.add)
                e.drain()
                ins = e.reciprocal(out=stv('rstd', NS), in_=stv('var', NS))
                ins.then_inc(p.sems['V'], 1)
            p.cnt['V'] += 1
            p.wait('scalar', 'V')
            if p.on('scalar'):
                ins = p.eng.activation(out=stv('rstd', NS), in_=stv('rstd', NS),
                                       func=AF.Sqrt)
                ins.then_inc(p.sems['A'], 1)
            p.cnt['A'] += 1
            p.wait('vector', 'A')
            if p.on('vector'):
                e = p.eng
                e.tensor_sub(out=stv('amax', NS), in0=stv('mx', NS), in1=stv('mean', NS))
                e.tensor_sub(out=stv('tmp', NS), in0=stv('mean', NS), in1=stv('mn', NS))
                e.drain()
                e.tensor_max(out=stv('amax', NS), in0=stv('amax', NS), in1=stv('tmp', NS))
                e.drain()
                e.tensor_mul(out=stv('amax', NS), in0=stv('amax', NS), in1=stv('rstd', NS))
                e.drain()
                e.tensor_scalar(out=stv('amax', NS), in0=stv('amax', NS), scalar1=1e-5,
                                scalar2=None, op0=ALU.max)
                e.drain()
                e.reciprocal(out=stv('sx', NS), in_=stv('amax', NS))
                e.drain()
                e.tensor_scalar(out=stv('sx', NS), in0=stv('sx', NS), scalar1=127.0,
                                scalar2=None, op0=ALU.mult)
                e.drain()
                e.reciprocal(out=stv('isx', NS), in_=stv('sx', NS))
                e.drain()
                e.tensor_mul(out=stv('qsc', NS), in0=stv('rstd', NS), in1=stv('sx', NS))
                e.drain()
                e.tensor_mul(out=stv('qb', NS), in0=stv('mean', NS), in1=stv('qsc', NS))
                e.drain()
                e.tensor_scalar(out=stv('qb', NS), in0=stv('qb', NS), scalar1=-1.0,
                                scalar2=None, op0=ALU.mult)
                e.drain()
                e.tensor_scalar(out=stv('gsc', NS), in0=stv('isx', NS),
                                scalar1=stc('mwg'), scalar2=None, op0=ALU.mult)
                e.drain()
                ins = e.tensor_scalar(out=stv('usc', NS), in0=stv('isx', NS),
                                      scalar1=stc('mwu'), scalar2=None, op0=ALU.mult)
                p.eng.drain()
                ins.then_inc(p.sems['V'], 1)
            p.cnt['V'] += 1

        def ffn_call(xb):
            ln_quant_stats(xb)
            for s in range(NS):
                p.wait('scalar', 'V')
                if p.on('scalar'):
                    ins = p.eng.activation(out=q1[:, (s % 2) * D:(s % 2 + 1) * D],
                                           in_=xb[:, s, :], func=AF.Identity,
                                           scale=stc('qsc', s), bias=stc('qb', s))
                    ins.then_inc(p.sems['A'], 1)
                p.cnt['A'] += 1
                p.wait('vector', 'A')
                p.wait('vector', 'PE')
                if p.on('vector'):
                    ins = p.eng.tensor_scalar(out=xqs[:, 0:D],
                                              in0=q1[:, (s % 2) * D:(s % 2 + 1) * D],
                                              scalar1=C_RNE, scalar2=C_RNE,
                                              op0=ALU.add, op1=ALU.subtract)
                    ins.then_inc(p.sems['V'], 1)
                p.cnt['V'] += 1
                ph = s % 2
                p.wait('tensor', 'V')
                pairs = [(psc[:, ph * 1024 + k * P:ph * 1024 + (k + 1) * P],
                          xqs[:, k * P:(k + 1) * P]) for k in range(8)]
                transpose_batch(p, pairs)
                p.wait('vector', 'PE')
                if p.on('vector'):
                    ins = p.eng.tensor_copy(
                        out=xqT[:, :, s * P:(s + 1) * P],
                        in_=psc[:, ph * 1024:(ph + 1) * 1024]
                            .rearrange("p (a b) -> p a b", b=P))
                    ins.then_inc(p.sems['V'], 1)
                p.cnt['V'] += 1

            if stop == 'dbgx':
                p.wait('sync', 'V')
                p.wait('sync', 'A')
                p.dma('sync', 'SD', out[0:128, 0:512],
                      xqT.rearrange("p a b -> p (a b)").bitcast(F32)[:, 0:512])
                p.dma('sync', 'SD', out[1024:1152, 0:256], st[:])
                p.dma('sync', 'SD', out[1280:1408, 0:1024], q1[:, 1024:2048])
                p.dma('sync', 'SD', out[1536:1664, 0:512],
                      xqs.bitcast(F32)[:, 0:512])
                p.wait('sync', 'SD')
                return

            def mm_phase(scr, dest_of, evac_eng):
                ev = {'n': 0}
                ch = 'V' if evac_eng == 'vector' else 'A'
                p.wait('tensor', 'V')
                p.wait('tensor', 'A')
                for oq in range(4):
                    p.wait('sync', 'PE', slot_rel[oq])
                    p.wait('sync', 'GD', cast_done_gd)
                    p.dma('sync', 'SD', ws[:, oq, :, :],
                          scr.rearrange("(ic p) o -> p ic o", p=P)
                          [:, :, oq * 1024:(oq + 1) * 1024])
                    p.wait('tensor', 'SD')
                    for s in range(NS):
                        for oh in range(2):
                            g = ev['n']
                            half = g % 2
                            if g >= 2:
                                p.wait('tensor', ch, ev[g - 2])
                            ops = [(xqT[:, ic, s * P:(s + 1) * P],
                                    ws[:, oq, ic, oh * 512:(oh + 1) * 512])
                                   for ic in range(8)]
                            mm_accum(p, psb[:, half * 512:(half + 1) * 512], ops)
                            p.wait(evac_eng, 'PE')
                            if p.on(evac_eng):
                                ins = dest_of(p.eng, s, oq, oh, half)
                                ins.then_inc(p.sems[ch], 1)
                            p.cnt[ch] += 1
                            ev[g] = p.cnt[ch]
                            ev['n'] += 1
                    slot_rel[oq] = p.cnt['PE']

            def gate_dest(e, s, oq, oh, half):
                sl = gu[:, s, oq * 1024 + oh * 512:oq * 1024 + (oh + 1) * 512]
                return e.activation(out=sl, in_=psb[:, half * 512:(half + 1) * 512],
                                    func=AF.Silu, scale=stc('gsc', s))

            def up_dest(e, s, oq, oh, half):
                sl = gu[:, s, oq * 1024 + oh * 512:oq * 1024 + (oh + 1) * 512]
                return e.scalar_tensor_tensor(out=sl,
                                              in0=psb[:, half * 512:(half + 1) * 512],
                                              scalar=stc('usc', s), in1=sl,
                                              op0=ALU.mult, op1=ALU.mult)

            mm_phase(s_wgT, gate_dest, 'scalar')
            mm_phase(s_wuT, up_dest, 'vector')

            p.wait('vector', 'V')
            p.drain('vector')
            if p.on('vector'):
                for s in range(NS):
                    ins = p.eng.reduce_max(out=stc('gamax', s), in_=gu[:, s, :],
                                           axis=AX.X, apply_absolute_value=True)
                ins.then_inc(p.sems['V'], 1)
            p.cnt['V'] += 1
            p.drain('vector')
            if p.on('vector'):
                e = p.eng
                e.drain()
                e.tensor_scalar(out=stv('gamax', NS), in0=stv('gamax', NS), scalar1=1e-5,
                                scalar2=None, op0=ALU.max)
                e.drain()
                e.reciprocal(out=stv('gsx', NS), in_=stv('gamax', NS))
                e.drain()
                e.tensor_scalar(out=stv('gsx', NS), in0=stv('gsx', NS), scalar1=127.0,
                                scalar2=None, op0=ALU.mult)
                e.drain()
                e.reciprocal(out=stv('gisx', NS), in_=stv('gsx', NS))
                e.drain()
                e.tensor_scalar(out=stv('dsc', NS), in0=stv('gisx', NS),
                                scalar1=stc('mwd'), scalar2=None, op0=ALU.mult)
                e.drain()
                ins = e.tensor_scalar(out=stv('dsc', NS), in0=stv('dsc', NS),
                                      scalar1=0.5, scalar2=None, op0=ALU.mult)
                e.drain()
                ins.then_inc(p.sems['V'], 1)
            p.cnt['V'] += 1
            for s in range(NS):
                p.wait('scalar', 'V')
                if p.on('scalar'):
                    ins = p.eng.activation(out=q1[:, 0:INNER], in_=gu[:, s, :],
                                           func=AF.Identity, scale=stc('gsx', s),
                                           bias=C_RNE)
                    ins.then_inc(p.sems['A'], 1)
                p.cnt['A'] += 1
                p.wait('vector', 'A')
                if p.on('vector'):
                    ins = p.eng.tensor_scalar(out=guq[s][:], in0=q1[:, 0:INNER],
                                              scalar1=C_RNE, scalar2=None,
                                              op0=ALU.subtract)
                    ins.then_inc(p.sems['V'], 1)
                p.cnt['V'] += 1

            if stop == 'dbgg':
                p.wait('sync', 'V')
                p.wait('sync', 'A')
                p.dma('sync', 'SD', out[0:128, 0:1024], gu[:, 0, 0:1024])
                p.dma('sync', 'SD', out[128:256, 0:1024],
                      guq[0].bitcast(F32)[:, 0:1024])
                p.dma('sync', 'SD', out[1024:1152, 0:256], st[:])
                p.wait('sync', 'SD')
                return

            for sl in range(4):
                p.wait('sync', 'PE', slot_rel[sl])
                p.dma('sync', 'SD', ws[:, sl, :, :],
                      s_wdT.rearrange("(ic p) o -> p ic o", p=P)[:, sl * 8:(sl + 1) * 8, :])
            p.wait('tensor', 'SD')
            dn = {'n': 0}
            for s in range(NS):
                for b in range(4):
                    ph = b % 2
                    p.wait('tensor', 'V')
                    pairs = [(psc[:, ph * 1024 + k * P:ph * 1024 + (k + 1) * P],
                              guq[s][:, (b * 8 + k) * P:(b * 8 + k + 1) * P])
                             for k in range(8)]
                    transpose_batch(p, pairs)
                    p.wait('vector', 'PE')
                    if p.on('vector'):
                        ins = p.eng.tensor_copy(
                            out=gqT[s % 2][:, b * 8:(b + 1) * 8, :],
                            in_=psc[:, ph * 1024:(ph + 1) * 1024]
                                .rearrange("p (a b) -> p a b", b=P))
                        ins.then_inc(p.sems['V'], 1)
                    p.cnt['V'] += 1
                p.wait('tensor', 'V')
                for oh in range(2):
                    g = dn['n']
                    half = g % 2
                    if g >= 2:
                        p.wait('tensor', 'V', dn[g - 2])
                    ops = [(gqT[s % 2][:, ic, :],
                            ws[:, ic // 8, ic % 8, oh * 512:(oh + 1) * 512])
                           for ic in range(32)]
                    mm_accum(p, psb[:, half * 512:(half + 1) * 512], ops)
                    p.wait('vector', 'PE')
                    if p.on('vector'):
                        ins = p.eng.scalar_tensor_tensor(
                            out=xb[:, s, oh * 512:(oh + 1) * 512],
                            in0=psb[:, half * 512:(half + 1) * 512],
                            scalar=stc('dsc', s),
                            in1=xb[:, s, oh * 512:(oh + 1) * 512],
                            op0=ALU.mult, op1=ALU.add)
                        ins.then_inc(p.sems['V'], 1)
                    p.cnt['V'] += 1
                    dn[g] = p.cnt['V']
                    dn['n'] += 1
            for sl in range(4):
                slot_rel[sl] = p.cnt['PE']

        def ln_to_lnT(xb, lnT, toff_sub):
            ln_quant_stats(xb)
            for s in range(NS):
                p.wait('vector', 'V')
                if p.on('vector'):
                    ins = p.eng.tensor_scalar(out=lns[:, 0:D], in0=xb[:, s, :],
                                              scalar1=stc('mean', s),
                                              scalar2=stc('rstd', s),
                                              op0=ALU.subtract, op1=ALU.mult)
                    ins.then_inc(p.sems['V'], 1)
                p.cnt['V'] += 1
                ph = s % 2
                p.wait('tensor', 'V')
                pairs = [(psc[:, ph * 1024 + k * P:ph * 1024 + (k + 1) * P],
                          lns[:, k * P:(k + 1) * P]) for k in range(8)]
                transpose_batch(p, pairs)
                p.wait('vector', 'PE')
                if p.on('vector'):
                    ins = p.eng.tensor_copy(
                        out=lnT[:, :, (toff_sub + s) * P:(toff_sub + s + 1) * P],
                        in_=psc[:, ph * 1024:(ph + 1) * 1024]
                            .rearrange("p (a b) -> p a b", b=P))
                    ins.then_inc(p.sems['V'], 1)
                p.cnt['V'] += 1

        def proj_fm(wslot, lnT, dst, ncol, tlen, scale_const=None, evac='scalar'):
            pe_phase_barrier()
            g = {'n': 0}
            ch = 'V' if evac == 'vector' else 'A'
            for oc in range(ncol):
                for th in range(tlen // 512):
                    gg = g['n']
                    half = gg % 2
                    if gg >= 2:
                        p.wait('tensor', ch, g[gg - 2])
                    ops = [(ws[:, wslot, ic, oc * P:(oc + 1) * P],
                            lnT[:, ic, th * 512:(th + 1) * 512]) for ic in range(8)]
                    mm_accum(p, psb[:, half * 512:(half + 1) * 512], ops)
                    p.wait(evac, 'PE')
                    if p.on(evac):
                        if evac == 'scalar':
                            ins = p.eng.activation(
                                out=dst[:, oc, th * 512:(th + 1) * 512],
                                in_=psb[:, half * 512:(half + 1) * 512],
                                func=AF.Copy,
                                scale=scale_const if scale_const is not None else 1.0)
                        else:
                            ins = p.eng.tensor_copy(
                                out=dst[:, oc, th * 512:(th + 1) * 512],
                                in_=psb[:, half * 512:(half + 1) * 512])
                        ins.then_inc(p.sems[ch], 1)
                    p.cnt[ch] += 1
                    g[gg] = p.cnt[ch]
                    g['n'] += 1

        def attn2(xb):
            ln_to_lnT(xb, lnT2, 0)
            if stop == 'a2ln':
                return
            p.wait('sync', 'PE', slot_rel[0])
            p.dma('sync', 'SD', ws[:, 0, :, :],
                  s_att['wq'].rearrange("(ic p) o -> p ic o", p=P))
            p.wait('sync', 'PE', slot_rel[1])
            p.dma('sync', 'SD', ws[:, 1, :, :],
                  s_att['wo'].rearrange("(ic p) o -> p ic o", p=P))
            p.wait('tensor', 'SD')
            proj_fm(0, lnT2, qt2, 8, T, scale_const=0.125, evac='scalar')
            slot_rel[0] = p.cnt['PE']
            if stop == 'a2q':
                return
            ng4 = 1 if stop in ('a2s0', 'a2s1', 'a2s2') else 4
            for s in range(NS if stop not in ('a2s', 'a2s0', 'a2s1', 'a2s2', 'a2s3') else 1):
                for g4 in range(ng4):
                    par = g4 % 2
                    sc = psa[:, par * 1024:(par + 1) * 1024]
                    if stop != 'a2s0':
                        p.wait('tensor', 'A')
                    for hh in range(4):
                        h = g4 * 4 + hh
                        po = (h % 2) * 64
                        sl = (hh % 2) * 2 + hh // 2
                        mm_accum(p, sc[:, sl * POOL:(sl + 1) * POOL],
                                 [(qt2[po:po + 64, h // 2, s * P:(s + 1) * P],
                                   kct[po:po + 64, h // 2, 0:POOL])],
                                 inc=(hh == 3))
                    if stop == 'a2s0':
                        return
                    p.wait('vector', 'PE')
                    if p.on('vector'):
                        for hh in range(4):
                            sl = (hh % 2) * 2 + hh // 2
                            ins = p.eng.reduce_max(
                                out=stc('negmax', g4 * 4 + hh),
                                in_=sc[:, sl * POOL:(sl + 1) * POOL],
                                axis=AX.X, negate=True)
                        p.eng.drain()
                        ins.then_inc(p.sems['V'], 1)
                    p.cnt['V'] += 1
                    if stop == 'a2s1':
                        return
                    p.wait('scalar', 'V')
                    p.wait('scalar', 'PE')
                    if p.on('scalar'):
                        for hh in range(4):
                            h = g4 * 4 + hh
                            sl = (hh % 2) * 2 + hh // 2
                            ins = p.eng.activation(out=pr2[:, h, :],
                                                   in_=sc[:, sl * POOL:(sl + 1) * POOL],
                                                   func=AF.Exp, bias=stc('negmax', h),
                                                   accum_out=stc('sume', h))
                        ins.then_inc(p.sems['A'], 1)
                    p.cnt['A'] += 1
                    if stop == 'a2s2':
                        return
                p.wait('vector', 'A')
                if p.on('vector'):
                    p.eng.reciprocal(out=stv('rsum', NH), in_=stv('sume', NH))
                    p.eng.drain()
                    for h in range(NH):
                        ins = p.eng.tensor_scalar(out=pr2[:, h, :], in0=pr2[:, h, :],
                                                  scalar1=stc('rsum', h), scalar2=None,
                                                  op0=ALU.mult)
                    p.eng.drain()
                    ins.then_inc(p.sems['V'], 1)
                p.cnt['V'] += 1
                for bb in range(2):
                    p.wait('tensor', 'V')
                    pairs = []
                    for hh in range(8):
                        h = bb * 8 + hh
                        for kc in range(2):
                            pairs.append(
                                (psc[:, (hh * 2 + kc) * P:(hh * 2 + kc + 1) * P],
                                 pr2[:, h, kc * P:(kc + 1) * P]))
                    transpose_batch(p, pairs)
                    p.wait('vector', 'PE')
                    if p.on('vector'):
                        ins = p.eng.tensor_copy(
                            out=prT2[:, :, bb * 8:(bb + 1) * 8, :],
                            in_=psc[:, 0:2048].rearrange("p (h k c) -> p k h c",
                                                         k=2, c=P))
                        ins.then_inc(p.sems['V'], 1)
                    p.cnt['V'] += 1
                if stop == 'a2s3':
                    return
                p.wait('tensor', 'V')
                pvv = psb.rearrange("p (a b) -> p a b", b=P)
                lastmm = None
                if p.on('tensor'):
                    for h in range(NH):
                        po = (h % 2) * 64
                        for kc in range(2):
                            lastmm = p.eng.matmul(pvv[po:po + 64, h // 2, :],
                                                  vc[:, kc, h * HD:(h + 1) * HD],
                                                  prT2[:, kc, h, :],
                                                  start=(kc == 0), stop=(kc == 1))
                p.inc('PE', lastmm)
                p.wait('vector', 'PE')
                if p.on('vector'):
                    ins = p.eng.tensor_copy(out=ot2[:, :, s * P:(s + 1) * P], in_=pvv[:])
                    ins.then_inc(p.sems['V'], 1)
                p.cnt['V'] += 1
            if stop in ('a2s', 'a2pv'):
                return
            g = {'n': 0}
            for s in range(NS):
                for oh in range(2):
                    gg = g['n']
                    half = gg % 2
                    p.wait('tensor', 'V', None if gg < 2 else g[gg - 2])
                    ops = [(ot2[:, ic, s * P:(s + 1) * P],
                            ws[:, 1, ic, oh * 512:(oh + 1) * 512]) for ic in range(8)]
                    mm_accum(p, psb[:, half * 512:(half + 1) * 512], ops)
                    p.wait('vector', 'PE')
                    if p.on('vector'):
                        ins = p.eng.tensor_tensor(
                            out=xb[:, s, oh * 512:(oh + 1) * 512],
                            in0=psb[:, half * 512:(half + 1) * 512],
                            in1=xb[:, s, oh * 512:(oh + 1) * 512], op=ALU.add)
                        ins.then_inc(p.sems['V'], 1)
                    p.cnt['V'] += 1
                    g[gg] = p.cnt['V']
                    g['n'] += 1
            slot_rel[1] = p.cnt['PE']

        def attn1():
            ln_to_lnT(xsb[0], lnT1, 0)
            ln_to_lnT(xsb[1], lnT1, 4)
            p.wait('sync', 'PE')
            p.wait('sync', 'GD', cast_done_gd)
            p.dma('sync', 'SD', ws[:, 0, :, 0:512],
                  s_att['w1'].rearrange("(ic p) o -> p ic o", p=P))
            p.dma('sync', 'SD', ws[:, 1, 0:4, :],
                  s_att['w2'].rearrange("(ic p) o -> p ic o", p=P))
            p.wait('tensor', 'SD')
            pe_phase_barrier()
            g = {'n': 0}
            for oc in range(4):
                for th in range(2):
                    gg = g['n']
                    half = gg % 2
                    if gg >= 2:
                        p.wait('tensor', 'A', g[gg - 2])
                    ops = [(ws[:, 0, ic, oc * P:(oc + 1) * P],
                            lnT1[:, ic, th * 512:(th + 1) * 512]) for ic in range(8)]
                    mm_accum(p, psb[:, half * 512:(half + 1) * 512], ops)
                    p.wait('scalar', 'PE')
                    if p.on('scalar'):
                        ins = p.eng.activation(out=relu1[:, oc, th * 512:(th + 1) * 512],
                                               in_=psb[:, half * 512:(half + 1) * 512],
                                               func=AF.Relu, bias=b1b[:, oc:oc + 1])
                        ins.then_inc(p.sems['A'], 1)
                    p.cnt['A'] += 1
                    g[gg] = p.cnt['A']
                    g['n'] += 1
            p.wait('vector', 'A')
            if p.on('vector'):
                for oc in range(4):
                    p.eng.tensor_reduce(out=q1[:, oc * POOL:(oc + 1) * POOL],
                                        in_=relu1[:, oc, :]
                                        .rearrange("p (n w) -> p n w", w=4),
                                        axis=AX.X, op=ALU.add)
                p.eng.drain()
                ins = p.eng.tensor_scalar(out=pool1.rearrange("p a b -> p (a b)"),
                                          in0=q1[:, 0:4 * POOL],
                                          scalar1=0.25, scalar2=None, op0=ALU.mult)
                p.eng.drain()
                ins.then_inc(p.sems['V'], 1)
            p.cnt['V'] += 1
            pe_phase_barrier()
            g = {'n': 0}
            for oc in range(8):
                gg = g['n']
                half = gg % 2
                if gg >= 2:
                    p.wait('tensor', 'V', g[gg - 2])
                ops = [(ws[:, 1, ic, oc * P:(oc + 1) * P], pool1[:, ic, :])
                       for ic in range(4)]
                mm_accum(p, psb[:, half * 512:half * 512 + POOL], ops)
                p.wait('vector', 'PE')
                if p.on('vector'):
                    ins = p.eng.tensor_scalar(out=compT[:, oc, :],
                                              in0=psb[:, half * 512:half * 512 + POOL],
                                              scalar1=b2b[:, oc:oc + 1], scalar2=None,
                                              op0=ALU.add)
                    ins.then_inc(p.sems['V'], 1)
                p.cnt['V'] += 1
                g[gg] = p.cnt['V']
                g['n'] += 1
            p.wait('sync', 'PE')
            p.dma('sync', 'SD', ws[:, 2, :, :],
                  s_att['wk'].rearrange("(ic p) o -> p ic o", p=P))
            p.dma('sync', 'SD', ws[:, 3, :, :],
                  s_att['wv'].rearrange("(ic p) o -> p ic o", p=P))
            p.wait('tensor', 'SD')
            for oc in range(8):
                gg = g['n']
                half = gg % 2
                p.wait('tensor', 'V', g[gg - 2])
                ops = [(ws[:, 2, ic, oc * P:(oc + 1) * P], compT[:, ic, :])
                       for ic in range(8)]
                mm_accum(p, psb[:, half * 512:half * 512 + POOL], ops)
                p.wait('vector', 'PE')
                if p.on('vector'):
                    ins = p.eng.tensor_copy(out=kct[:, oc, :],
                                            in_=psb[:, half * 512:half * 512 + POOL])
                    ins.then_inc(p.sems['V'], 1)
                p.cnt['V'] += 1
                g[gg] = p.cnt['V']
                g['n'] += 1
            for kc in range(2):
                for oh in range(2):
                    gg = g['n']
                    half = gg % 2
                    p.wait('tensor', 'V', g[gg - 2])
                    ops = [(compT[:, ic, kc * P:(kc + 1) * P],
                            ws[:, 3, ic, oh * 512:(oh + 1) * 512]) for ic in range(8)]
                    mm_accum(p, psb[:, half * 512:(half + 1) * 512], ops)
                    p.wait('vector', 'PE')
                    if p.on('vector'):
                        ins = p.eng.tensor_copy(out=vc[:, kc, oh * 512:(oh + 1) * 512],
                                                in_=psb[:, half * 512:(half + 1) * 512])
                        ins.then_inc(p.sems['V'], 1)
                    p.cnt['V'] += 1
                    g[gg] = p.cnt['V']
                    g['n'] += 1
            if stop == 'dbga':
                p.wait('sync', 'V')
                p.dma('sync', 'SD', out[0:128, 0:1024],
                      compT.rearrange("p a b -> p (a b)").bitcast(F32))
                p.dma('sync', 'SD', out[128:256, 0:1024],
                      kct.rearrange("p a b -> p (a b)").bitcast(F32))
                p.dma('sync', 'SD', out[256:384, 0:1024],
                      vc.rearrange("p a b -> p (a b)").bitcast(F32))
                p.dma('sync', 'SD', out[384:512, 0:512],
                      lnT1.rearrange("p a b -> p (a b)").bitcast(F32)[:, 0:512])
                p.wait('sync', 'SD')
                return
            p.wait('sync', 'PE')
            p.dma('sync', 'SD', ws[:, 0, :, :],
                  s_att['wq'].rearrange("(ic p) o -> p ic o", p=P))
            p.wait('tensor', 'SD')
            proj_fm(0, lnT1, qt1, 8, CTX, scale_const=0.125, evac='scalar')
            proj_fm(2, lnT1, kt1, 8, CTX, evac='vector')
            pe_phase_barrier()
            g = {'n': 0}
            for s8 in range(8):
                for oh in range(2):
                    gg = g['n']
                    half = gg % 2
                    if gg >= 2:
                        p.wait('tensor', 'V', g[gg - 2])
                    ops = [(lnT1[:, ic, s8 * P:(s8 + 1) * P],
                            ws[:, 3, ic, oh * 512:(oh + 1) * 512]) for ic in range(8)]
                    mm_accum(p, psb[:, half * 512:(half + 1) * 512], ops)
                    p.wait('vector', 'PE')
                    if p.on('vector'):
                        ins = p.eng.tensor_copy(out=v1[:, s8, oh * 512:(oh + 1) * 512],
                                                in_=psb[:, half * 512:(half + 1) * 512])
                        ins.then_inc(p.sems['V'], 1)
                    p.cnt['V'] += 1
                    g[gg] = p.cnt['V']
                    g['n'] += 1
            pe_phase_barrier()
            for qs in range(8):
                kend = (qs + 1) * P
                for h in range(NH):
                    par = h % 2
                    sc = psa[:, par * 1024:par * 1024 + kend]
                    p.wait('tensor', 'A')
                    po = (h % 2) * 64
                    lastmm = None
                    if p.on('tensor'):
                        nk = (kend + 511) // 512
                        for kk in range(nk):
                            k0, k1 = kk * 512, min(kend, (kk + 1) * 512)
                            lastmm = p.eng.matmul(
                                sc[:, k0:k1],
                                qt1[po:po + 64, h // 2, qs * P:(qs + 1) * P],
                                kt1[po:po + 64, h // 2, k0:k1],
                                start=True, stop=True)
                    p.inc('PE', lastmm)
                    p.wait('vector', 'PE')
                    if p.on('vector'):
                        p.eng.tensor_tensor(out=sc[:, qs * P:(qs + 1) * P],
                                            in0=sc[:, qs * P:(qs + 1) * P],
                                            in1=maskb[:], op=ALU.add)
                        p.eng.drain()
                        ins = p.eng.reduce_max(out=stc('negmax', h), in_=sc[:, 0:kend],
                                               axis=AX.X, negate=True)
                        p.eng.drain()
                        ins.then_inc(p.sems['V'], 1)
                    p.cnt['V'] += 1
                    p.wait('scalar', 'V')
                    p.wait('scalar', 'PE')
                    if p.on('scalar'):
                        ins = p.eng.activation(out=pr1[h % 2][:, 0:kend],
                                               in_=sc[:, 0:kend], func=AF.Exp,
                                               bias=stc('negmax', h),
                                               accum_out=stc('sume', h))
                        ins.then_inc(p.sems['A'], 1)
                    p.cnt['A'] += 1
                    p.wait('vector', 'A')
                    if p.on('vector'):
                        p.eng.reciprocal(out=stc('rsum', h), in_=stc('sume', h))
                        p.eng.drain()
                        ins = p.eng.tensor_scalar(out=pr1[h % 2][:, 0:kend],
                                                  in0=pr1[h % 2][:, 0:kend],
                                                  scalar1=stc('rsum', h), scalar2=None,
                                                  op0=ALU.mult)
                        p.eng.drain()
                        ins.then_inc(p.sems['V'], 1)
                    p.cnt['V'] += 1
                    p.wait('tensor', 'V')
                    pairs = [(psc[:, kc * P:(kc + 1) * P],
                              pr1[h % 2][:, kc * P:(kc + 1) * P])
                             for kc in range(qs + 1)]
                    transpose_batch(p, pairs)
                    p.wait('vector', 'PE')
                    if p.on('vector'):
                        ins = p.eng.tensor_copy(
                            out=prT1[h % 2][:, 0:qs + 1, :],
                            in_=psc[:, 0:kend].rearrange("p (a b) -> p a b", b=P))
                        ins.then_inc(p.sems['V'], 1)
                    p.cnt['V'] += 1
                    p.wait('tensor', 'V')
                    pvv = psb.rearrange("p (a b) -> p a b", b=P)
                    lastmm = None
                    if p.on('tensor'):
                        for kc in range(qs + 1):
                            lastmm = p.eng.matmul(pvv[po:po + 64, h // 2, :],
                                                  v1[:, kc, h * HD:(h + 1) * HD],
                                                  prT1[h % 2][:, kc, :],
                                                  start=(kc == 0), stop=(kc == qs))
                    p.inc('PE', lastmm)
                p.wait('vector', 'PE')
                if p.on('vector'):
                    ins = p.eng.tensor_copy(out=ot1[:, :, qs * P:(qs + 1) * P],
                                            in_=psb.rearrange("p (a b) -> p a b", b=P))
                    ins.then_inc(p.sems['V'], 1)
                p.cnt['V'] += 1
            if stop == 'dbgb':
                p.wait('sync', 'V')
                p.wait('sync', 'A')
                for nm, buf in [(0, qt1), (1, kt1), (2, v1), (3, ot1)]:
                    flat = buf.rearrange("p a b -> p (a b)").bitcast(F32)
                    for c in range(4):
                        p.dma('sync', 'SD', out[nm * 512 + c * 128:nm * 512 + (c + 1) * 128, :],
                              flat[:, c * 1024:(c + 1) * 1024])
                p.wait('sync', 'SD')
                return
            p.wait('sync', 'PE')
            p.dma('sync', 'SD', ws[:, 1, :, :],
                  s_att['wo'].rearrange("(ic p) o -> p ic o", p=P))
            p.wait('tensor', 'SD')
            pe_phase_barrier()
            g = {'n': 0}
            for s8 in range(8):
                xb = xsb[s8 // 4]
                s = s8 % 4
                for oh in range(2):
                    gg = g['n']
                    half = gg % 2
                    if gg >= 2:
                        p.wait('tensor', 'V', g[gg - 2])
                    ops = [(ot1[:, ic, s8 * P:(s8 + 1) * P],
                            ws[:, 1, ic, oh * 512:(oh + 1) * 512]) for ic in range(8)]
                    mm_accum(p, psb[:, half * 512:(half + 1) * 512], ops)
                    p.wait('vector', 'PE')
                    if p.on('vector'):
                        ins = p.eng.tensor_tensor(
                            out=xb[:, s, oh * 512:(oh + 1) * 512],
                            in0=psb[:, half * 512:(half + 1) * 512],
                            in1=xb[:, s, oh * 512:(oh + 1) * 512], op=ALU.add)
                        ins.then_inc(p.sems['V'], 1)
                    p.cnt['V'] += 1
                    g[gg] = p.cnt['V']
                    g['n'] += 1
            for sl in range(4):
                slot_rel[sl] = p.cnt['PE']

        def full_barrier():
            for ek in ('tensor', 'vector', 'scalar', 'sync', 'gpsimd'):
                for ch in ('PE', 'V', 'A'):
                    p.wait(ek, ch)

        def store_tile(t, xb):
            p.wait('gpsimd', 'V')
            p.dma('gpsimd', 'GD',
                  out[t * T:(t + 1) * T, :].rearrange("(s p) d -> p s d", p=P), xb[:])

        ffn_call(xsb[0])
        if stop in ('dbgx', 'dbgg'):
            return
        if stop == 'ffn1t0':
            store_tile(0, xsb[0])
            p.wait('gpsimd', 'GD')
            return
        if stop == 'ffn1b2':
            ffn_call(xsb[1])
            store_tile(0, xsb[0])
            store_tile(1, xsb[1])
            p.wait('gpsimd', 'GD')
            return
        ffn_call(xsb[1])
        if stop == 'ffn1both':
            store_tile(0, xsb[0])
            store_tile(1, xsb[1])
            p.wait('gpsimd', 'GD')
            return
        attn1()
        if stop in ('dbga', 'dbgb'):
            return
        if stop == 'attn1':
            store_tile(0, xsb[0])
            store_tile(1, xsb[1])
            p.wait('gpsimd', 'GD')
            return
        ffn_call(xsb[0])
        ffn_call(xsb[1])
        store_tile(0, xsb[0])
        store_tile(1, xsb[1])
        if stop == 'ffn2':
            p.wait('gpsimd', 'GD')
            return
        for t in range(2, NT):
            bi = t % 2
            p.wait('gpsimd', 'GD')
            p.dma('gpsimd', 'GD', xsb[bi][:],
                  x_in[t * T:(t + 1) * T, :].rearrange("(s p) d -> p s d", p=P))
            ffn_call(xsb[bi])
            if stop != 'tile2nf':
                attn2(xsb[bi])
            if stop in ('a2ln', 'a2q', 'a2s', 'a2pv', 'a2s0', 'a2s1', 'a2s2', 'a2s3'):
                store_tile(t, xsb[bi])
                p.wait('gpsimd', 'GD')
                return
            ffn_call(xsb[bi])
            store_tile(t, xsb[bi])
            if stop in ('tile2', 'tile2nf') and t == 2:
                p.wait('gpsimd', 'GD')
                return
        p.wait('gpsimd', 'GD')

    walks = {}

    def run_walk(ek, eng):
        w = Walk(nc, ek, eng, sems)
        program(w)
        walks[ek] = w

    block.sync(lambda eng: run_walk('sync', eng))
    block.gpsimd(lambda eng: run_walk('gpsimd', eng))
    block.tensor(lambda eng: run_walk('tensor', eng))
    block.vector(lambda eng: run_walk('vector', eng))
    block.scalar(lambda eng: run_walk('scalar', eng))

    cnts = [w.cnt for w in walks.values()]
    for c in cnts[1:]:
        assert c == cnts[0], f"walk divergence: {cnts[0]} vs {c}"

    ctx.close()
    return nc


_NC_CACHE = {}


def kernel(L0_attn_hidden_states, L0_attn_Wq, L0_attn_Wk, L0_attn_Wv, L0_attn_Wo,
           L0_attn_W1, L0_attn_b1, L0_attn_W2, L0_attn_b2,
           L0_ffn_w_gate, L0_ffn_w_up, L0_ffn_w_down):
    hs = np.ascontiguousarray(np.asarray(L0_attn_hidden_states, dtype=np.float32))
    wmap = {k: np.ascontiguousarray(np.asarray(v, dtype=np.float32)) for k, v in [
        ("wq", L0_attn_Wq), ("wk", L0_attn_Wk), ("wv", L0_attn_Wv), ("wo", L0_attn_Wo),
        ("w1", L0_attn_W1), ("b1", L0_attn_b1), ("w2", L0_attn_W2), ("b2", L0_attn_b2),
        ("wg", L0_ffn_w_gate), ("wu", L0_ffn_w_up), ("wd", L0_ffn_w_down)]}
    if 'nc' not in _NC_CACHE:
        _NC_CACHE['nc'] = build_program()
    nc = _NC_CACHE['nc']
    in_maps = [dict(wmap, x=hs[c]) for c in range(8)]
    res = run_bass_kernel_spmd(nc, in_maps, core_ids=list(range(8)))
    return np.stack([r["out"] for r in res.results], axis=0)


# version 04
# speedup vs baseline: 1.0358x; 1.0358x over previous
import numpy as np
import ml_dtypes
import concourse.bass as bass
import concourse.mybir as mybir
from concourse.bass_utils import run_bass_kernel_spmd

F32 = mybir.dt.float32
BF16 = mybir.dt.bfloat16
AF = mybir.ActivationFunctionType
ALU = mybir.AluOpType
AX = mybir.AxisListType

P = 128
D = 1024
S = 4096
T = 512
NS = T // P
NT = S // T
INNER = 4096
NH = 16
HD = 64
CTX = 1024
POOL = 256
C_RNE = float(np.float32(3 * 2.0 ** 22))
EPS = 1e-5

SBUF_BASE = 16512
OFF_WS = SBUF_BASE
OFF_GU = SBUF_BASE + 65536
OFF_Q1 = SBUF_BASE + 131072
OFF_XQT = SBUF_BASE + 147456
OFF_XQS = SBUF_BASE + 155648
OFF_X0 = SBUF_BASE + 157696
OFF_X1 = SBUF_BASE + 174080
OFF_KCT = SBUF_BASE + 190464
OFF_VC = SBUF_BASE + 194560
OFF_STAT = SBUF_BASE + 198656
OFF_CONST = SBUF_BASE + 199680


def carve(nc, name, shape, dtype, off):
    h = nc.virtual_sbuf_tensor(name, shape, dtype)
    nc._tensor(h.name, list(h.shape), h.dtype, type="SB")
    mloc = nc.lookup_mloc(h)
    mloc.allocated = not nc.target_bir_lowering
    mloc.addr = off
    return h


class Walk:

    def __init__(self, nc, ek, eng, sems):
        self.nc = nc
        self.ek = ek
        self.eng = eng
        self.sems = sems
        self.cnt = {k: 0 for k in sems}

    def inc(self, ch, instr, n=1):
        self.cnt[ch] += n
        if instr is not None:
            instr.then_inc(self.sems[ch], n)

    def dma(self, ek, ch, out, in_, **kw):
        self.cnt[ch] += 16
        if self.ek == ek:
            self.eng.dma_start(out=out, in_=in_, **kw).then_inc(self.sems[ch], 16)

    def wait(self, ek, ch, val=None):
        if self.ek == ek:
            self.eng.wait_ge(self.sems[ch], self.cnt[ch] if val is None else val)

    def on(self, ek):
        return self.ek == ek

    def drain(self, ek):
        if self.ek == ek:
            self.eng.drain()


def build_program(stop=None):
    nc = bass.Bass("TRN2", target_bir_lowering=False)

    x_in = nc.dram_tensor("x", [S, D], F32, kind="ExternalInput").ap()
    wq_in = nc.dram_tensor("wq", [D, D], F32, kind="ExternalInput").ap()
    wk_in = nc.dram_tensor("wk", [D, D], F32, kind="ExternalInput").ap()
    wv_in = nc.dram_tensor("wv", [D, D], F32, kind="ExternalInput").ap()
    wo_in = nc.dram_tensor("wo", [D, D], F32, kind="ExternalInput").ap()
    w1_in = nc.dram_tensor("w1", [D, 512], F32, kind="ExternalInput").ap()
    b1_in = nc.dram_tensor("b1", [512], F32, kind="ExternalInput").ap()
    w2_in = nc.dram_tensor("w2", [512, D], F32, kind="ExternalInput").ap()
    b2_in = nc.dram_tensor("b2", [D], F32, kind="ExternalInput").ap()
    wg_in = nc.dram_tensor("wg", [INNER, D], F32, kind="ExternalInput").ap()
    wu_in = nc.dram_tensor("wu", [INNER, D], F32, kind="ExternalInput").ap()
    wd_in = nc.dram_tensor("wd", [D, INNER], F32, kind="ExternalInput").ap()
    out = nc.dram_tensor("out", [S, D], F32, kind="ExternalOutput").ap()

    s_wgT = nc.dram_tensor("s_wgT", [D, INNER], BF16, kind="Internal").ap()
    s_wuT = nc.dram_tensor("s_wuT", [D, INNER], BF16, kind="Internal").ap()
    s_wdT = nc.dram_tensor("s_wdT", [INNER, D], BF16, kind="Internal").ap()
    s_att = {}
    for k, rows, cols in [("wq", D, D), ("wk", D, D), ("wv", D, D), ("wo", D, D),
                          ("w1", D, 512), ("w2", 512, D)]:
        s_att[k] = nc.dram_tensor("s_" + k, [rows, cols], BF16, kind="Internal").ap()

    identb_t = nc.inline_tensor(np.eye(P, dtype=ml_dtypes.bfloat16), name="c_identb")
    identf_t = nc.inline_tensor(np.eye(P, dtype=np.float32), name="c_identf")
    maskc = np.where(np.arange(P)[None, :] <= np.arange(P)[:, None],
                     np.float32(0.0), np.float32(-1e30)).astype(np.float32)
    mask_t = nc.inline_tensor(maskc, name="c_mask")
    ones_t = nc.inline_tensor(np.ones((1, P), dtype=np.float32), name="c_ones")
    cvals = np.broadcast_to(np.array([[C_RNE, 0.125]], dtype=np.float32), (P, 2)).copy()
    cval_t = nc.inline_tensor(cvals, name="c_cvals")

    ws = carve(nc, "ws", [P, 4, 8, 1024], BF16, OFF_WS)
    gu = carve(nc, "gu", [P, NS, INNER], F32, OFF_GU)
    guq = [carve(nc, f"guq{s}", [P, INNER], BF16, OFF_GU + s * 16384) for s in range(NS)]
    q1 = carve(nc, "q1", [P, 4096], F32, OFF_Q1)
    gqT = [carve(nc, f"gqT{i}", [P, 32, P], BF16, OFF_Q1 + i * 8192) for i in range(2)]
    xqT = carve(nc, "xqT", [P, 8, T], BF16, OFF_XQT)
    xqs = carve(nc, "xqs", [P, D], BF16, OFF_XQS)
    xsb = [carve(nc, "x0", [P, NS, D], F32, OFF_X0),
           carve(nc, "x1", [P, NS, D], F32, OFF_X1)]
    kct = carve(nc, "kct", [P, 8, POOL], BF16, OFF_KCT)
    vc = carve(nc, "vc", [P, 2, D], BF16, OFF_VC)
    st = carve(nc, "st", [P, 256], F32, OFF_STAT)
    identb = carve(nc, "identb", [P, P], BF16, OFF_CONST)
    identf = carve(nc, "identf", [P, P], F32, OFF_CONST + 256)
    maskb = carve(nc, "maskb", [P, P], F32, OFF_CONST + 768)
    onesb = carve(nc, "onesb", [1, P], F32, OFF_CONST + 1280)
    b1b = carve(nc, "b1b", [P, 4], F32, OFF_CONST + 1792)
    b2b = carve(nc, "b2b", [P, 8], F32, OFF_CONST + 1808)
    cvalb = carve(nc, "cvalb", [P, 2], F32, OFF_CONST + 1840)
    lnT2 = carve(nc, "lnT2", [P, 8, T], BF16, OFF_GU + 0)
    qt2 = carve(nc, "qt2", [P, 8, T], BF16, OFF_GU + 8192)
    ot2 = carve(nc, "ot2", [P, 8, T], BF16, OFF_GU + 16384)
    pr2 = carve(nc, "pr2", [P, NH, POOL], BF16, OFF_GU + 24576)
    prT2 = carve(nc, "prT2", [P, 2, NH, P], BF16, OFF_GU + 32768)
    lns = carve(nc, "lns", [P, D], BF16, OFF_GU + 40960)
    lnT1 = carve(nc, "lnT1", [P, 8, CTX], BF16, OFF_GU + 0)
    qt1 = carve(nc, "qt1", [P, 8, CTX], BF16, OFF_GU + 16384)
    kt1 = carve(nc, "kt1", [P, 8, CTX], BF16, OFF_GU + 32768)
    v1 = carve(nc, "v1", [P, 8, D], BF16, OFF_GU + 49152)
    ot1 = carve(nc, "ot1", [P, 8, CTX], BF16, OFF_Q1 + 0)
    relu1 = carve(nc, "relu1", [P, 4, CTX], BF16, OFF_XQT)
    pool1 = carve(nc, "pool1", [P, 4, POOL], BF16, OFF_XQS)
    compT = carve(nc, "compT", [P, 8, POOL], BF16, OFF_XQT)
    pr1 = [carve(nc, f"pr1_{i}", [P, CTX], BF16, OFF_XQT + i * 2048) for i in range(2)]
    prT1 = [carve(nc, f"prT1_{i}", [P, 8, P], BF16, OFF_XQT + 4096 + i * 2048)
            for i in range(2)]
    wfg = carve(nc, "wfg", [P, 32, D], F32, OFF_WS)
    wfd = carve(nc, "wfd", [P, 8, INNER], F32, OFF_WS)
    wqj = carve(nc, "wqj", [P, 4096], BF16, OFF_KCT)
    wcol = carve(nc, "wcol", [P, 8, 1024], BF16, OFF_Q1 + 4096)
    wdcol = carve(nc, "wdcol", [P, 32, P], BF16, OFF_XQT)

    nc.const_aps.register(F32, C_RNE, cvalb[:, 0:1])
    nc.const_aps.register(F32, 0.125, cvalb[:, 1:2])

    COL = dict(sums=0, sumsq=4, mx=8, mn=12, mean=16, m2=20, var=24, rstd=28,
               amax=32, sx=36, isx=40, qsc=44, qb=48, gsc=52, usc=56,
               gamax=60, gsx=64, gisx=68, dsc=72, tmp=76,
               negmax=80, sume=96, rsum=112,
               wtmp=128, swrep=224, mwg=225, mwu=226, mwd=227)

    def stc(name, s=0):
        c = COL[name] + s
        return st[:, c:c + 1]

    def stv(name, w, s=0):
        c = COL[name] + s
        return st[:, c:c + w]

    from contextlib import ExitStack
    ctx = ExitStack()
    sems = {k: ctx.enter_context(nc.semaphore(k)) for k in ['SD', 'GD', 'PE', 'V', 'A']}
    psa = ctx.enter_context(nc.psum_tensor("psa", [P, 2048], F32))
    psb = ctx.enter_context(nc.psum_tensor("psb", [P, 1024], F32))
    psc = ctx.enter_context(nc.psum_tensor("psc", [P, 2048], BF16))
    block = ctx.enter_context(nc.Block())

    def program(p: Walk):
        slot_rel = {0: 0, 1: 0, 2: 0, 3: 0}

        def mm_accum(p_, out_ap, ops, inc=True):
            last = None
            if p_.on('tensor'):
                n = len(ops)
                for i, (l, r) in enumerate(ops):
                    last = p_.eng.matmul(out_ap, l, r, start=(i == 0), stop=(i == n - 1))
            if inc:
                p_.inc('PE', last)
            return last

        def transpose_batch(p_, pairs, inc=True):
            last = None
            if p_.on('tensor'):
                for o, i in pairs:
                    last = p_.eng.transpose(o, i, identb[:])
            if inc:
                p_.inc('PE', last)
            return last

        def pe_phase_barrier():
            p.wait('tensor', 'V')
            p.wait('tensor', 'A')

        p.dma('sync', 'SD', identb[:], identb_t.ap())
        p.dma('sync', 'SD', identf[:], identf_t.ap())
        p.dma('sync', 'SD', maskb[:], mask_t.ap())
        p.dma('sync', 'SD', onesb[:], ones_t.ap())
        p.dma('sync', 'SD', cvalb[:], cval_t.ap())
        p.dma('sync', 'SD', b1b[:], b1_in.rearrange("(c p) -> p c", p=P),
              allow_slow_non_contiguous=True)
        p.dma('sync', 'SD', b2b[:], b2_in.rearrange("(c p) -> p c", p=P),
              allow_slow_non_contiguous=True)
        p.dma('gpsimd', 'GD', xsb[0][:], x_in[0:T, :].rearrange("(s p) d -> p s d", p=P))
        p.dma('gpsimd', 'GD', xsb[1][:], x_in[T:2 * T, :].rearrange("(s p) d -> p s d", p=P))

        def quant_weight(w_src, n_rows, scr_T, mw_col):
            rl = D if n_rows == 32 else INNER
            wf = wfg if n_rows == 32 else wfd
            nelem = float(n_rows * P * rl)
            p.wait('sync', 'A')
            p.dma('sync', 'SD', wf[:], w_src.rearrange("(j p) i -> p j i", p=P))
            p.wait('scalar', 'SD')
            if p.on('scalar'):
                for j in range(n_rows):
                    ins = p.eng.activation(out=q1[:, 0:rl], in_=wf[:, j, :], func=AF.Abs,
                                           accum_out=stv('wtmp', 1, j))
                ins.then_inc(p.sems['A'], 1)
            p.cnt['A'] += 1
            p.wait('vector', 'A')
            if p.on('vector'):
                ins = p.eng.reduce_sum(out=stv('wtmp', 1, 64), in_=stv('wtmp', n_rows, 0),
                                       axis=AX.X)
                ins.then_inc(p.sems['V'], 1)
                p.eng.drain()
            p.cnt['V'] += 1
            p.wait('tensor', 'V')
            if p.on('tensor'):
                ins = p.eng.transpose(psa[0:1, 0:P], stv('wtmp', 1, 64), identf[:])
            p.inc('PE', ins if p.on('tensor') else None)
            p.wait('vector', 'PE')
            if p.on('vector'):
                e = p.eng
                e.tensor_copy(out=st[0:1, 0:P], in_=psa[0:1, 0:P])
                e.drain()
                c65 = st[0:1, COL['wtmp'] + 65:COL['wtmp'] + 66]
                c66 = st[0:1, COL['wtmp'] + 66:COL['wtmp'] + 67]
                c67 = st[0:1, COL['wtmp'] + 67:COL['wtmp'] + 68]
                e.reduce_sum(out=c65, in_=st[0:1, 0:P], axis=AX.X)
                e.drain()
                e.tensor_scalar(out=c65, in0=c65, scalar1=1.0 / nelem, scalar2=1e-5,
                                op0=ALU.mult, op1=ALU.max)
                e.drain()
                e.reciprocal(out=c66, in_=c65)
                e.drain()
                ins = e.reciprocal(out=c67, in_=c66)
                ins.then_inc(p.sems['V'], 1)
                e.drain()
            p.cnt['V'] += 1
            p.wait('tensor', 'V')
            if p.on('tensor'):
                ins = p.eng.matmul(psa[:, 256:258], onesb[0:1, 0:P],
                                   st[0:1, COL['wtmp'] + 66:COL['wtmp'] + 68],
                                   start=True, stop=True)
            p.inc('PE', ins if p.on('tensor') else None)
            p.wait('vector', 'PE')
            if p.on('vector'):
                p.eng.tensor_copy(out=stc('swrep'), in_=psa[:, 256:257])
                ins = p.eng.tensor_copy(out=st[:, mw_col:mw_col + 1], in_=psa[:, 257:258])
                p.eng.drain()
                ins.then_inc(p.sems['V'], 1)
            p.cnt['V'] += 1

            for j in range(n_rows):
                p.wait('scalar', 'V')
                if p.on('scalar'):
                    ins = p.eng.activation(out=q1[:, 0:rl], in_=wf[:, j, :],
                                           func=AF.Identity, scale=stc('swrep'),
                                           bias=C_RNE)
                    ins.then_inc(p.sems['A'], 1)
                p.cnt['A'] += 1
                p.wait('vector', 'A')
                p.wait('vector', 'PE')
                if n_rows == 32 and j % 8 == 0:
                    p.wait('vector', 'GD')
                if p.on('vector'):
                    p.eng.tensor_scalar(out=q1[:, 0:rl], in0=q1[:, 0:rl], scalar1=C_RNE,
                                        scalar2=1.0, op0=ALU.subtract, op1=ALU.min)
                    p.eng.drain()
                    ins = p.eng.tensor_scalar(out=wqj[:, 0:rl], in0=q1[:, 0:rl],
                                              scalar1=-1.0, scalar2=None, op0=ALU.max)
                    ins.then_inc(p.sems['V'], 1)
                p.cnt['V'] += 1
                ntp = rl // P
                p.wait('tensor', 'V')
                for b in range(ntp // 8):
                    ph = b % 2
                    p.wait('tensor', 'V')
                    pairs = [(psc[:, ph * 1024 + k * P:ph * 1024 + (k + 1) * P],
                              wqj[:, (b * 8 + k) * P:(b * 8 + k + 1) * P])
                             for k in range(8)]
                    transpose_batch(p, pairs)
                    p.wait('vector', 'PE')
                    if n_rows == 8 and b == 0:
                        p.wait('vector', 'GD')
                    if p.on('vector'):
                        src = psc[:, ph * 1024:(ph + 1) * 1024].rearrange(
                            "p (a b) -> p a b", b=P)
                        if n_rows == 32:
                            ins = p.eng.tensor_copy(
                                out=wcol[:, :, (j % 8) * P:(j % 8 + 1) * P], in_=src)
                        else:
                            ins = p.eng.tensor_copy(
                                out=wdcol[:, b * 8:(b + 1) * 8, :], in_=src)
                        ins.then_inc(p.sems['V'], 1)
                    p.cnt['V'] += 1
                if n_rows == 32 and j % 8 == 7:
                    jg = j // 8
                    p.wait('gpsimd', 'V')
                    p.dma('gpsimd', 'GD',
                          scr_T.rearrange("(ic p) o -> p ic o", p=P)[:, :, jg * 1024:(jg + 1) * 1024],
                          wcol[:])
                elif n_rows == 8:
                    p.wait('gpsimd', 'V')
                    p.dma('gpsimd', 'GD',
                          scr_T.rearrange("(ic p) o -> p ic o", p=P)[:, :, j * P:(j + 1) * P],
                          wdcol[:])

        quant_weight(wg_in, 32, s_wgT, COL['mwg'])
        if stop == 'dbgq':
            p.wait('sync', 'GD')
            p.dma('sync', 'SD', out[0:1024, 0:1024],
                  s_wgT.bitcast(F32)[:, 0:1024])
            p.wait('sync', 'V')
            p.wait('sync', 'A')
            p.dma('sync', 'SD', out[1024:1152, 0:256], st[:])
            p.wait('sync', 'SD')
            return
        quant_weight(wu_in, 32, s_wuT, COL['mwu'])
        quant_weight(wd_in, 8, s_wdT, COL['mwd'])

        p.wait('gpsimd', 'A')
        p.wait('gpsimd', 'V')
        p.wait('gpsimd', 'PE')
        ws_flat = ws.rearrange("p a b c -> p (a b) c")
        for key, src, rows, cols in [("wq", wq_in, D, D), ("wk", wk_in, D, D),
                                     ("wv", wv_in, D, D), ("wo", wo_in, D, D),
                                     ("w1", w1_in, D, 512), ("w2", w2_in, 512, D)]:
            p.wait('gpsimd', 'GD')
            p.dma('gpsimd', 'GD', ws_flat[:, 0:rows // P, 0:cols],
                  src.rearrange("(j p) o -> p j o", p=P))
            p.wait('gpsimd', 'GD')
            p.dma('gpsimd', 'GD', s_att[key].rearrange("(j p) o -> p j o", p=P),
                  ws_flat[:, 0:rows // P, 0:cols])
        cast_done_gd = p.cnt['GD']


        def ln_quant_stats(xb):
            p.wait('vector', 'GD')
            p.wait('scalar', 'GD')
            p.wait('scalar', 'PE')
            p.wait('scalar', 'V')
            p.drain('vector')
            if p.on('vector'):
                for s in range(NS):
                    p.eng.reduce_sum(out=stc('sums', s), in_=xb[:, s, :], axis=AX.X)
                    p.eng.reduce_max(out=stc('mx', s), in_=xb[:, s, :], axis=AX.X)
                    ins = p.eng.tensor_reduce(out=stc('mn', s), in_=xb[:, s, :],
                                              axis=AX.X, op=ALU.min)
                ins.then_inc(p.sems['V'], 1)
            p.cnt['V'] += 1
            if p.on('scalar'):
                for s in range(NS):
                    ins = p.eng.activation(out=q1[:, 0:D], in_=xb[:, s, :],
                                           func=AF.Square, accum_out=stc('sumsq', s))
                ins.then_inc(p.sems['A'], 1)
            p.cnt['A'] += 1
            p.wait('vector', 'A')
            if p.on('vector'):
                e = p.eng
                e.drain()
                e.tensor_scalar(out=stv('mean', NS), in0=stv('sums', NS),
                                scalar1=1.0 / D, scalar2=None, op0=ALU.mult)
                e.drain()
                e.tensor_mul(out=stv('m2', NS), in0=stv('mean', NS), in1=stv('mean', NS))
                e.drain()
                e.scalar_tensor_tensor(out=stv('var', NS), in0=stv('sumsq', NS),
                                       scalar=1.0 / D, in1=stv('m2', NS),
                                       op0=ALU.mult, op1=ALU.subtract)
                e.drain()
                e.tensor_scalar(out=stv('var', NS), in0=stv('var', NS), scalar1=EPS,
                                scalar2=None, op0=ALU.add)
                e.drain()
                ins = e.reciprocal(out=stv('rstd', NS), in_=stv('var', NS))
                ins.then_inc(p.sems['V'], 1)
            p.cnt['V'] += 1
            p.wait('scalar', 'V')
            if p.on('scalar'):
                ins = p.eng.activation(out=stv('rstd', NS), in_=stv('rstd', NS),
                                       func=AF.Sqrt)
                ins.then_inc(p.sems['A'], 1)
            p.cnt['A'] += 1
            p.wait('vector', 'A')
            if p.on('vector'):
                e = p.eng
                e.tensor_sub(out=stv('amax', NS), in0=stv('mx', NS), in1=stv('mean', NS))
                e.tensor_sub(out=stv('tmp', NS), in0=stv('mean', NS), in1=stv('mn', NS))
                e.drain()
                e.tensor_max(out=stv('amax', NS), in0=stv('amax', NS), in1=stv('tmp', NS))
                e.drain()
                e.tensor_mul(out=stv('amax', NS), in0=stv('amax', NS), in1=stv('rstd', NS))
                e.drain()
                e.tensor_scalar(out=stv('amax', NS), in0=stv('amax', NS), scalar1=1e-5,
                                scalar2=None, op0=ALU.max)
                e.drain()
                e.reciprocal(out=stv('sx', NS), in_=stv('amax', NS))
                e.drain()
                e.tensor_scalar(out=stv('sx', NS), in0=stv('sx', NS), scalar1=127.0,
                                scalar2=None, op0=ALU.mult)
                e.drain()
                e.reciprocal(out=stv('isx', NS), in_=stv('sx', NS))
                e.drain()
                e.tensor_mul(out=stv('qsc', NS), in0=stv('rstd', NS), in1=stv('sx', NS))
                e.drain()
                e.tensor_mul(out=stv('qb', NS), in0=stv('mean', NS), in1=stv('qsc', NS))
                e.drain()
                e.tensor_scalar(out=stv('qb', NS), in0=stv('qb', NS), scalar1=-1.0,
                                scalar2=None, op0=ALU.mult)
                e.drain()
                e.tensor_scalar(out=stv('gsc', NS), in0=stv('isx', NS),
                                scalar1=stc('mwg'), scalar2=None, op0=ALU.mult)
                e.drain()
                ins = e.tensor_scalar(out=stv('usc', NS), in0=stv('isx', NS),
                                      scalar1=stc('mwu'), scalar2=None, op0=ALU.mult)
                p.eng.drain()
                ins.then_inc(p.sems['V'], 1)
            p.cnt['V'] += 1

        def ffn_call(xb):
            ln_quant_stats(xb)
            for s in range(NS):
                p.wait('scalar', 'V')
                if p.on('scalar'):
                    ins = p.eng.activation(out=q1[:, (s % 2) * D:(s % 2 + 1) * D],
                                           in_=xb[:, s, :], func=AF.Identity,
                                           scale=stc('qsc', s), bias=stc('qb', s))
                    ins.then_inc(p.sems['A'], 1)
                p.cnt['A'] += 1
                p.wait('vector', 'A')
                p.wait('vector', 'PE')
                if p.on('vector'):
                    ins = p.eng.tensor_scalar(out=xqs[:, 0:D],
                                              in0=q1[:, (s % 2) * D:(s % 2 + 1) * D],
                                              scalar1=C_RNE, scalar2=C_RNE,
                                              op0=ALU.add, op1=ALU.subtract)
                    ins.then_inc(p.sems['V'], 1)
                p.cnt['V'] += 1
                ph = s % 2
                p.wait('tensor', 'V')
                pairs = [(psc[:, ph * 1024 + k * P:ph * 1024 + (k + 1) * P],
                          xqs[:, k * P:(k + 1) * P]) for k in range(8)]
                transpose_batch(p, pairs)
                p.wait('vector', 'PE')
                if p.on('vector'):
                    ins = p.eng.tensor_copy(
                        out=xqT[:, :, s * P:(s + 1) * P],
                        in_=psc[:, ph * 1024:(ph + 1) * 1024]
                            .rearrange("p (a b) -> p a b", b=P))
                    ins.then_inc(p.sems['V'], 1)
                p.cnt['V'] += 1

            if stop == 'dbgx':
                p.wait('sync', 'V')
                p.wait('sync', 'A')
                p.dma('sync', 'SD', out[0:128, 0:512],
                      xqT.rearrange("p a b -> p (a b)").bitcast(F32)[:, 0:512])
                p.dma('sync', 'SD', out[1024:1152, 0:256], st[:])
                p.dma('sync', 'SD', out[1280:1408, 0:1024], q1[:, 1024:2048])
                p.dma('sync', 'SD', out[1536:1664, 0:512],
                      xqs.bitcast(F32)[:, 0:512])
                p.wait('sync', 'SD')
                return

            def mm_phase(scr, dest_of, evac_eng):
                ev = {'n': 0}
                ch = 'V' if evac_eng == 'vector' else 'A'
                p.wait('tensor', 'V')
                p.wait('tensor', 'A')
                for oq in range(4):
                    p.wait('sync', 'PE', slot_rel[oq])
                    p.wait('sync', 'GD')
                    p.dma('sync', 'SD', ws[:, oq, :, :],
                          scr.rearrange("(ic p) o -> p ic o", p=P)
                          [:, :, oq * 1024:(oq + 1) * 1024])
                    p.wait('tensor', 'SD')
                    for s in range(NS):
                        for oh in range(2):
                            g = ev['n']
                            half = g % 2
                            if g >= 2:
                                p.wait('tensor', ch, ev[g - 2])
                            ops = [(xqT[:, ic, s * P:(s + 1) * P],
                                    ws[:, oq, ic, oh * 512:(oh + 1) * 512])
                                   for ic in range(8)]
                            mm_accum(p, psb[:, half * 512:(half + 1) * 512], ops)
                            p.wait(evac_eng, 'PE')
                            if p.on(evac_eng):
                                ins = dest_of(p.eng, s, oq, oh, half)
                                ins.then_inc(p.sems[ch], 1)
                            p.cnt[ch] += 1
                            ev[g] = p.cnt[ch]
                            ev['n'] += 1
                    slot_rel[oq] = p.cnt['PE']

            def gate_dest(e, s, oq, oh, half):
                sl = gu[:, s, oq * 1024 + oh * 512:oq * 1024 + (oh + 1) * 512]
                return e.activation(out=sl, in_=psb[:, half * 512:(half + 1) * 512],
                                    func=AF.Silu, scale=stc('gsc', s))

            def up_dest(e, s, oq, oh, half):
                sl = gu[:, s, oq * 1024 + oh * 512:oq * 1024 + (oh + 1) * 512]
                return e.scalar_tensor_tensor(out=sl,
                                              in0=psb[:, half * 512:(half + 1) * 512],
                                              scalar=stc('usc', s), in1=sl,
                                              op0=ALU.mult, op1=ALU.mult)

            mm_phase(s_wgT, gate_dest, 'scalar')
            mm_phase(s_wuT, up_dest, 'vector')

            p.wait('vector', 'V')
            p.drain('vector')
            if p.on('vector'):
                for s in range(NS):
                    ins = p.eng.reduce_max(out=stc('gamax', s), in_=gu[:, s, :],
                                           axis=AX.X, apply_absolute_value=True)
                ins.then_inc(p.sems['V'], 1)
            p.cnt['V'] += 1
            p.drain('vector')
            if p.on('vector'):
                e = p.eng
                e.drain()
                e.tensor_scalar(out=stv('gamax', NS), in0=stv('gamax', NS), scalar1=1e-5,
                                scalar2=None, op0=ALU.max)
                e.drain()
                e.reciprocal(out=stv('gsx', NS), in_=stv('gamax', NS))
                e.drain()
                e.tensor_scalar(out=stv('gsx', NS), in0=stv('gsx', NS), scalar1=127.0,
                                scalar2=None, op0=ALU.mult)
                e.drain()
                e.reciprocal(out=stv('gisx', NS), in_=stv('gsx', NS))
                e.drain()
                e.tensor_scalar(out=stv('dsc', NS), in0=stv('gisx', NS),
                                scalar1=stc('mwd'), scalar2=None, op0=ALU.mult)
                e.drain()
                ins = e.tensor_scalar(out=stv('dsc', NS), in0=stv('dsc', NS),
                                      scalar1=0.5, scalar2=None, op0=ALU.mult)
                e.drain()
                ins.then_inc(p.sems['V'], 1)
            p.cnt['V'] += 1
            for s in range(NS):
                p.wait('scalar', 'V')
                if p.on('scalar'):
                    ins = p.eng.activation(out=q1[:, 0:INNER], in_=gu[:, s, :],
                                           func=AF.Identity, scale=stc('gsx', s),
                                           bias=C_RNE)
                    ins.then_inc(p.sems['A'], 1)
                p.cnt['A'] += 1
                p.wait('vector', 'A')
                if p.on('vector'):
                    ins = p.eng.tensor_scalar(out=guq[s][:], in0=q1[:, 0:INNER],
                                              scalar1=C_RNE, scalar2=None,
                                              op0=ALU.subtract)
                    ins.then_inc(p.sems['V'], 1)
                p.cnt['V'] += 1

            if stop == 'dbgg':
                p.wait('sync', 'V')
                p.wait('sync', 'A')
                p.dma('sync', 'SD', out[0:128, 0:1024], gu[:, 0, 0:1024])
                p.dma('sync', 'SD', out[128:256, 0:1024],
                      guq[0].bitcast(F32)[:, 0:1024])
                p.dma('sync', 'SD', out[1024:1152, 0:256], st[:])
                p.wait('sync', 'SD')
                return

            for sl in range(4):
                p.wait('sync', 'PE', slot_rel[sl])
                p.dma('sync', 'SD', ws[:, sl, :, :],
                      s_wdT.rearrange("(ic p) o -> p ic o", p=P)[:, sl * 8:(sl + 1) * 8, :])
            p.wait('tensor', 'SD')
            dn = {'n': 0}
            for s in range(NS):
                for b in range(4):
                    ph = b % 2
                    p.wait('tensor', 'V')
                    pairs = [(psc[:, ph * 1024 + k * P:ph * 1024 + (k + 1) * P],
                              guq[s][:, (b * 8 + k) * P:(b * 8 + k + 1) * P])
                             for k in range(8)]
                    transpose_batch(p, pairs)
                    p.wait('vector', 'PE')
                    if p.on('vector'):
                        ins = p.eng.tensor_copy(
                            out=gqT[s % 2][:, b * 8:(b + 1) * 8, :],
                            in_=psc[:, ph * 1024:(ph + 1) * 1024]
                                .rearrange("p (a b) -> p a b", b=P))
                        ins.then_inc(p.sems['V'], 1)
                    p.cnt['V'] += 1
                p.wait('tensor', 'V')
                for oh in range(2):
                    g = dn['n']
                    half = g % 2
                    if g >= 2:
                        p.wait('tensor', 'V', dn[g - 2])
                    ops = [(gqT[s % 2][:, ic, :],
                            ws[:, ic // 8, ic % 8, oh * 512:(oh + 1) * 512])
                           for ic in range(32)]
                    mm_accum(p, psb[:, half * 512:(half + 1) * 512], ops)
                    p.wait('vector', 'PE')
                    if p.on('vector'):
                        ins = p.eng.scalar_tensor_tensor(
                            out=xb[:, s, oh * 512:(oh + 1) * 512],
                            in0=psb[:, half * 512:(half + 1) * 512],
                            scalar=stc('dsc', s),
                            in1=xb[:, s, oh * 512:(oh + 1) * 512],
                            op0=ALU.mult, op1=ALU.add)
                        ins.then_inc(p.sems['V'], 1)
                    p.cnt['V'] += 1
                    dn[g] = p.cnt['V']
                    dn['n'] += 1
            for sl in range(4):
                slot_rel[sl] = p.cnt['PE']

        def ln_to_lnT(xb, lnT, toff_sub):
            ln_quant_stats(xb)
            for s in range(NS):
                p.wait('vector', 'V')
                if p.on('vector'):
                    ins = p.eng.tensor_scalar(out=lns[:, 0:D], in0=xb[:, s, :],
                                              scalar1=stc('mean', s),
                                              scalar2=stc('rstd', s),
                                              op0=ALU.subtract, op1=ALU.mult)
                    ins.then_inc(p.sems['V'], 1)
                p.cnt['V'] += 1
                ph = s % 2
                p.wait('tensor', 'V')
                pairs = [(psc[:, ph * 1024 + k * P:ph * 1024 + (k + 1) * P],
                          lns[:, k * P:(k + 1) * P]) for k in range(8)]
                transpose_batch(p, pairs)
                p.wait('vector', 'PE')
                if p.on('vector'):
                    ins = p.eng.tensor_copy(
                        out=lnT[:, :, (toff_sub + s) * P:(toff_sub + s + 1) * P],
                        in_=psc[:, ph * 1024:(ph + 1) * 1024]
                            .rearrange("p (a b) -> p a b", b=P))
                    ins.then_inc(p.sems['V'], 1)
                p.cnt['V'] += 1

        def proj_fm(wslot, lnT, dst, ncol, tlen, scale_const=None, evac='scalar'):
            pe_phase_barrier()
            g = {'n': 0}
            ch = 'V' if evac == 'vector' else 'A'
            for oc in range(ncol):
                for th in range(tlen // 512):
                    gg = g['n']
                    half = gg % 2
                    if gg >= 2:
                        p.wait('tensor', ch, g[gg - 2])
                    ops = [(ws[:, wslot, ic, oc * P:(oc + 1) * P],
                            lnT[:, ic, th * 512:(th + 1) * 512]) for ic in range(8)]
                    mm_accum(p, psb[:, half * 512:(half + 1) * 512], ops)
                    p.wait(evac, 'PE')
                    if p.on(evac):
                        if evac == 'scalar':
                            ins = p.eng.activation(
                                out=dst[:, oc, th * 512:(th + 1) * 512],
                                in_=psb[:, half * 512:(half + 1) * 512],
                                func=AF.Copy,
                                scale=scale_const if scale_const is not None else 1.0)
                        else:
                            ins = p.eng.tensor_copy(
                                out=dst[:, oc, th * 512:(th + 1) * 512],
                                in_=psb[:, half * 512:(half + 1) * 512])
                        ins.then_inc(p.sems[ch], 1)
                    p.cnt[ch] += 1
                    g[gg] = p.cnt[ch]
                    g['n'] += 1

        def attn2(xb):
            ln_to_lnT(xb, lnT2, 0)
            if stop == 'a2ln':
                return
            p.wait('sync', 'PE', slot_rel[0])
            p.dma('sync', 'SD', ws[:, 0, :, :],
                  s_att['wq'].rearrange("(ic p) o -> p ic o", p=P))
            p.wait('sync', 'PE', slot_rel[1])
            p.dma('sync', 'SD', ws[:, 1, :, :],
                  s_att['wo'].rearrange("(ic p) o -> p ic o", p=P))
            p.wait('tensor', 'SD')
            proj_fm(0, lnT2, qt2, 8, T, scale_const=0.125, evac='scalar')
            slot_rel[0] = p.cnt['PE']
            if stop == 'a2q':
                return
            a_rel = {0: 0, 1: 0}
            ng4 = 1 if stop in ('a2s0', 'a2s1', 'a2s2') else 4
            for s in range(NS if stop not in ('a2s', 'a2s0', 'a2s1', 'a2s2', 'a2s3') else 1):
                for g4 in range(ng4):
                    par = g4 % 2
                    sc = psa[:, par * 1024:(par + 1) * 1024]
                    if stop != 'a2s0':
                        p.wait('tensor', 'A', a_rel[par])
                    for hh in range(4):
                        h = g4 * 4 + hh
                        po = (h % 2) * 64
                        sl = (hh % 2) * 2 + hh // 2
                        mm_accum(p, sc[:, sl * POOL:(sl + 1) * POOL],
                                 [(qt2[po:po + 64, h // 2, s * P:(s + 1) * P],
                                   kct[po:po + 64, h // 2, 0:POOL])],
                                 inc=(hh == 3))
                    if stop == 'a2s0':
                        return
                    p.wait('vector', 'PE')
                    if p.on('vector'):
                        for hh in range(4):
                            sl = (hh % 2) * 2 + hh // 2
                            ins = p.eng.reduce_max(
                                out=stc('negmax', g4 * 4 + hh),
                                in_=sc[:, sl * POOL:(sl + 1) * POOL],
                                axis=AX.X, negate=True)
                        p.eng.drain()
                        ins.then_inc(p.sems['V'], 1)
                    p.cnt['V'] += 1
                    if stop == 'a2s1':
                        return
                    p.wait('scalar', 'V')
                    p.wait('scalar', 'PE')
                    if p.on('scalar'):
                        for hh in range(4):
                            h = g4 * 4 + hh
                            sl = (hh % 2) * 2 + hh // 2
                            ins = p.eng.activation(out=pr2[:, h, :],
                                                   in_=sc[:, sl * POOL:(sl + 1) * POOL],
                                                   func=AF.Exp, bias=stc('negmax', h),
                                                   accum_out=stc('sume', h))
                        ins.then_inc(p.sems['A'], 1)
                    p.cnt['A'] += 1
                    a_rel[par] = p.cnt['A']
                    if stop == 'a2s2':
                        return
                p.wait('vector', 'A')
                if p.on('vector'):
                    p.eng.reciprocal(out=stv('rsum', NH), in_=stv('sume', NH))
                    p.eng.drain()
                    for h in range(NH):
                        ins = p.eng.tensor_scalar(out=pr2[:, h, :], in0=pr2[:, h, :],
                                                  scalar1=stc('rsum', h), scalar2=None,
                                                  op0=ALU.mult)
                    p.eng.drain()
                    ins.then_inc(p.sems['V'], 1)
                p.cnt['V'] += 1
                for bb in range(2):
                    p.wait('tensor', 'V')
                    pairs = []
                    for hh in range(8):
                        h = bb * 8 + hh
                        for kc in range(2):
                            pairs.append(
                                (psc[:, (hh * 2 + kc) * P:(hh * 2 + kc + 1) * P],
                                 pr2[:, h, kc * P:(kc + 1) * P]))
                    transpose_batch(p, pairs)
                    p.wait('vector', 'PE')
                    if p.on('vector'):
                        ins = p.eng.tensor_copy(
                            out=prT2[:, :, bb * 8:(bb + 1) * 8, :],
                            in_=psc[:, 0:2048].rearrange("p (h k c) -> p k h c",
                                                         k=2, c=P))
                        ins.then_inc(p.sems['V'], 1)
                    p.cnt['V'] += 1
                if stop == 'a2s3':
                    return
                p.wait('tensor', 'V')
                pvv = psb.rearrange("p (a b) -> p a b", b=P)
                lastmm = None
                if p.on('tensor'):
                    for h in range(NH):
                        po = (h % 2) * 64
                        for kc in range(2):
                            lastmm = p.eng.matmul(pvv[po:po + 64, h // 2, :],
                                                  vc[:, kc, h * HD:(h + 1) * HD],
                                                  prT2[:, kc, h, :],
                                                  start=(kc == 0), stop=(kc == 1))
                p.inc('PE', lastmm)
                p.wait('vector', 'PE')
                if p.on('vector'):
                    ins = p.eng.tensor_copy(out=ot2[:, :, s * P:(s + 1) * P], in_=pvv[:])
                    ins.then_inc(p.sems['V'], 1)
                p.cnt['V'] += 1
            if stop in ('a2s', 'a2pv'):
                return
            g = {'n': 0}
            for s in range(NS):
                for oh in range(2):
                    gg = g['n']
                    half = gg % 2
                    p.wait('tensor', 'V', None if gg < 2 else g[gg - 2])
                    ops = [(ot2[:, ic, s * P:(s + 1) * P],
                            ws[:, 1, ic, oh * 512:(oh + 1) * 512]) for ic in range(8)]
                    mm_accum(p, psb[:, half * 512:(half + 1) * 512], ops)
                    p.wait('vector', 'PE')
                    if p.on('vector'):
                        ins = p.eng.tensor_tensor(
                            out=xb[:, s, oh * 512:(oh + 1) * 512],
                            in0=psb[:, half * 512:(half + 1) * 512],
                            in1=xb[:, s, oh * 512:(oh + 1) * 512], op=ALU.add)
                        ins.then_inc(p.sems['V'], 1)
                    p.cnt['V'] += 1
                    g[gg] = p.cnt['V']
                    g['n'] += 1
            slot_rel[1] = p.cnt['PE']

        def attn1():
            ln_to_lnT(xsb[0], lnT1, 0)
            ln_to_lnT(xsb[1], lnT1, 4)
            p.wait('sync', 'PE')
            p.wait('sync', 'GD', cast_done_gd)
            p.dma('sync', 'SD', ws[:, 0, :, 0:512],
                  s_att['w1'].rearrange("(ic p) o -> p ic o", p=P))
            p.dma('sync', 'SD', ws[:, 1, 0:4, :],
                  s_att['w2'].rearrange("(ic p) o -> p ic o", p=P))
            p.wait('tensor', 'SD')
            pe_phase_barrier()
            g = {'n': 0}
            for oc in range(4):
                for th in range(2):
                    gg = g['n']
                    half = gg % 2
                    if gg >= 2:
                        p.wait('tensor', 'A', g[gg - 2])
                    ops = [(ws[:, 0, ic, oc * P:(oc + 1) * P],
                            lnT1[:, ic, th * 512:(th + 1) * 512]) for ic in range(8)]
                    mm_accum(p, psb[:, half * 512:(half + 1) * 512], ops)
                    p.wait('scalar', 'PE')
                    if p.on('scalar'):
                        ins = p.eng.activation(out=relu1[:, oc, th * 512:(th + 1) * 512],
                                               in_=psb[:, half * 512:(half + 1) * 512],
                                               func=AF.Relu, bias=b1b[:, oc:oc + 1])
                        ins.then_inc(p.sems['A'], 1)
                    p.cnt['A'] += 1
                    g[gg] = p.cnt['A']
                    g['n'] += 1
            p.wait('vector', 'A')
            if p.on('vector'):
                for oc in range(4):
                    p.eng.tensor_reduce(out=q1[:, oc * POOL:(oc + 1) * POOL],
                                        in_=relu1[:, oc, :]
                                        .rearrange("p (n w) -> p n w", w=4),
                                        axis=AX.X, op=ALU.add)
                p.eng.drain()
                ins = p.eng.tensor_scalar(out=pool1.rearrange("p a b -> p (a b)"),
                                          in0=q1[:, 0:4 * POOL],
                                          scalar1=0.25, scalar2=None, op0=ALU.mult)
                p.eng.drain()
                ins.then_inc(p.sems['V'], 1)
            p.cnt['V'] += 1
            pe_phase_barrier()
            g = {'n': 0}
            for oc in range(8):
                gg = g['n']
                half = gg % 2
                if gg >= 2:
                    p.wait('tensor', 'V', g[gg - 2])
                ops = [(ws[:, 1, ic, oc * P:(oc + 1) * P], pool1[:, ic, :])
                       for ic in range(4)]
                mm_accum(p, psb[:, half * 512:half * 512 + POOL], ops)
                p.wait('vector', 'PE')
                if p.on('vector'):
                    ins = p.eng.tensor_scalar(out=compT[:, oc, :],
                                              in0=psb[:, half * 512:half * 512 + POOL],
                                              scalar1=b2b[:, oc:oc + 1], scalar2=None,
                                              op0=ALU.add)
                    ins.then_inc(p.sems['V'], 1)
                p.cnt['V'] += 1
                g[gg] = p.cnt['V']
                g['n'] += 1
            p.wait('sync', 'PE')
            p.dma('sync', 'SD', ws[:, 2, :, :],
                  s_att['wk'].rearrange("(ic p) o -> p ic o", p=P))
            p.dma('sync', 'SD', ws[:, 3, :, :],
                  s_att['wv'].rearrange("(ic p) o -> p ic o", p=P))
            p.wait('tensor', 'SD')
            for oc in range(8):
                gg = g['n']
                half = gg % 2
                p.wait('tensor', 'V', g[gg - 2])
                ops = [(ws[:, 2, ic, oc * P:(oc + 1) * P], compT[:, ic, :])
                       for ic in range(8)]
                mm_accum(p, psb[:, half * 512:half * 512 + POOL], ops)
                p.wait('vector', 'PE')
                if p.on('vector'):
                    ins = p.eng.tensor_copy(out=kct[:, oc, :],
                                            in_=psb[:, half * 512:half * 512 + POOL])
                    ins.then_inc(p.sems['V'], 1)
                p.cnt['V'] += 1
                g[gg] = p.cnt['V']
                g['n'] += 1
            for kc in range(2):
                for oh in range(2):
                    gg = g['n']
                    half = gg % 2
                    p.wait('tensor', 'V', g[gg - 2])
                    ops = [(compT[:, ic, kc * P:(kc + 1) * P],
                            ws[:, 3, ic, oh * 512:(oh + 1) * 512]) for ic in range(8)]
                    mm_accum(p, psb[:, half * 512:(half + 1) * 512], ops)
                    p.wait('vector', 'PE')
                    if p.on('vector'):
                        ins = p.eng.tensor_copy(out=vc[:, kc, oh * 512:(oh + 1) * 512],
                                                in_=psb[:, half * 512:(half + 1) * 512])
                        ins.then_inc(p.sems['V'], 1)
                    p.cnt['V'] += 1
                    g[gg] = p.cnt['V']
                    g['n'] += 1
            if stop == 'dbga':
                p.wait('sync', 'V')
                p.dma('sync', 'SD', out[0:128, 0:1024],
                      compT.rearrange("p a b -> p (a b)").bitcast(F32))
                p.dma('sync', 'SD', out[128:256, 0:1024],
                      kct.rearrange("p a b -> p (a b)").bitcast(F32))
                p.dma('sync', 'SD', out[256:384, 0:1024],
                      vc.rearrange("p a b -> p (a b)").bitcast(F32))
                p.dma('sync', 'SD', out[384:512, 0:512],
                      lnT1.rearrange("p a b -> p (a b)").bitcast(F32)[:, 0:512])
                p.wait('sync', 'SD')
                return
            p.wait('sync', 'PE')
            p.dma('sync', 'SD', ws[:, 0, :, :],
                  s_att['wq'].rearrange("(ic p) o -> p ic o", p=P))
            p.wait('tensor', 'SD')
            proj_fm(0, lnT1, qt1, 8, CTX, scale_const=0.125, evac='scalar')
            proj_fm(2, lnT1, kt1, 8, CTX, evac='vector')
            pe_phase_barrier()
            g = {'n': 0}
            for s8 in range(8):
                for oh in range(2):
                    gg = g['n']
                    half = gg % 2
                    if gg >= 2:
                        p.wait('tensor', 'V', g[gg - 2])
                    ops = [(lnT1[:, ic, s8 * P:(s8 + 1) * P],
                            ws[:, 3, ic, oh * 512:(oh + 1) * 512]) for ic in range(8)]
                    mm_accum(p, psb[:, half * 512:(half + 1) * 512], ops)
                    p.wait('vector', 'PE')
                    if p.on('vector'):
                        ins = p.eng.tensor_copy(out=v1[:, s8, oh * 512:(oh + 1) * 512],
                                                in_=psb[:, half * 512:(half + 1) * 512])
                        ins.then_inc(p.sems['V'], 1)
                    p.cnt['V'] += 1
                    g[gg] = p.cnt['V']
                    g['n'] += 1
            pe_phase_barrier()
            a1_rel = {0: 0, 1: 0}
            for qs in range(8):
                kend = (qs + 1) * P
                for h in range(NH):
                    par = h % 2
                    sc = psa[:, par * 1024:par * 1024 + kend]
                    p.wait('tensor', 'A', a1_rel[par])
                    po = (h % 2) * 64
                    lastmm = None
                    if p.on('tensor'):
                        nk = (kend + 511) // 512
                        for kk in range(nk):
                            k0, k1 = kk * 512, min(kend, (kk + 1) * 512)
                            lastmm = p.eng.matmul(
                                sc[:, k0:k1],
                                qt1[po:po + 64, h // 2, qs * P:(qs + 1) * P],
                                kt1[po:po + 64, h // 2, k0:k1],
                                start=True, stop=True)
                    p.inc('PE', lastmm)
                    p.wait('vector', 'PE')
                    if p.on('vector'):
                        p.eng.tensor_tensor(out=sc[:, qs * P:(qs + 1) * P],
                                            in0=sc[:, qs * P:(qs + 1) * P],
                                            in1=maskb[:], op=ALU.add)
                        p.eng.drain()
                        ins = p.eng.reduce_max(out=stc('negmax', h), in_=sc[:, 0:kend],
                                               axis=AX.X, negate=True)
                        p.eng.drain()
                        ins.then_inc(p.sems['V'], 1)
                    p.cnt['V'] += 1
                    p.wait('scalar', 'V')
                    p.wait('scalar', 'PE')
                    if p.on('scalar'):
                        ins = p.eng.activation(out=pr1[h % 2][:, 0:kend],
                                               in_=sc[:, 0:kend], func=AF.Exp,
                                               bias=stc('negmax', h),
                                               accum_out=stc('sume', h))
                        ins.then_inc(p.sems['A'], 1)
                    p.cnt['A'] += 1
                    a1_rel[par] = p.cnt['A']
                    p.wait('vector', 'A')
                    if p.on('vector'):
                        p.eng.reciprocal(out=stc('rsum', h), in_=stc('sume', h))
                        p.eng.drain()
                        ins = p.eng.tensor_scalar(out=pr1[h % 2][:, 0:kend],
                                                  in0=pr1[h % 2][:, 0:kend],
                                                  scalar1=stc('rsum', h), scalar2=None,
                                                  op0=ALU.mult)
                        p.eng.drain()
                        ins.then_inc(p.sems['V'], 1)
                    p.cnt['V'] += 1
                    p.wait('tensor', 'V')
                    pairs = [(psc[:, kc * P:(kc + 1) * P],
                              pr1[h % 2][:, kc * P:(kc + 1) * P])
                             for kc in range(qs + 1)]
                    transpose_batch(p, pairs)
                    p.wait('vector', 'PE')
                    if p.on('vector'):
                        ins = p.eng.tensor_copy(
                            out=prT1[h % 2][:, 0:qs + 1, :],
                            in_=psc[:, 0:kend].rearrange("p (a b) -> p a b", b=P))
                        ins.then_inc(p.sems['V'], 1)
                    p.cnt['V'] += 1
                    p.wait('tensor', 'V')
                    pvv = psb.rearrange("p (a b) -> p a b", b=P)
                    lastmm = None
                    if p.on('tensor'):
                        for kc in range(qs + 1):
                            lastmm = p.eng.matmul(pvv[po:po + 64, h // 2, :],
                                                  v1[:, kc, h * HD:(h + 1) * HD],
                                                  prT1[h % 2][:, kc, :],
                                                  start=(kc == 0), stop=(kc == qs))
                    p.inc('PE', lastmm)
                p.wait('vector', 'PE')
                if p.on('vector'):
                    ins = p.eng.tensor_copy(out=ot1[:, :, qs * P:(qs + 1) * P],
                                            in_=psb.rearrange("p (a b) -> p a b", b=P))
                    ins.then_inc(p.sems['V'], 1)
                p.cnt['V'] += 1
            if stop == 'dbgb':
                p.wait('sync', 'V')
                p.wait('sync', 'A')
                for nm, buf in [(0, qt1), (1, kt1), (2, v1), (3, ot1)]:
                    flat = buf.rearrange("p a b -> p (a b)").bitcast(F32)
                    for c in range(4):
                        p.dma('sync', 'SD', out[nm * 512 + c * 128:nm * 512 + (c + 1) * 128, :],
                              flat[:, c * 1024:(c + 1) * 1024])
                p.wait('sync', 'SD')
                return
            p.wait('sync', 'PE')
            p.dma('sync', 'SD', ws[:, 1, :, :],
                  s_att['wo'].rearrange("(ic p) o -> p ic o", p=P))
            p.wait('tensor', 'SD')
            pe_phase_barrier()
            g = {'n': 0}
            for s8 in range(8):
                xb = xsb[s8 // 4]
                s = s8 % 4
                for oh in range(2):
                    gg = g['n']
                    half = gg % 2
                    if gg >= 2:
                        p.wait('tensor', 'V', g[gg - 2])
                    ops = [(ot1[:, ic, s8 * P:(s8 + 1) * P],
                            ws[:, 1, ic, oh * 512:(oh + 1) * 512]) for ic in range(8)]
                    mm_accum(p, psb[:, half * 512:(half + 1) * 512], ops)
                    p.wait('vector', 'PE')
                    if p.on('vector'):
                        ins = p.eng.tensor_tensor(
                            out=xb[:, s, oh * 512:(oh + 1) * 512],
                            in0=psb[:, half * 512:(half + 1) * 512],
                            in1=xb[:, s, oh * 512:(oh + 1) * 512], op=ALU.add)
                        ins.then_inc(p.sems['V'], 1)
                    p.cnt['V'] += 1
                    g[gg] = p.cnt['V']
                    g['n'] += 1
            for sl in range(4):
                slot_rel[sl] = p.cnt['PE']

        def full_barrier():
            for ek in ('tensor', 'vector', 'scalar', 'sync', 'gpsimd'):
                for ch in ('PE', 'V', 'A'):
                    p.wait(ek, ch)

        def store_tile(t, xb):
            p.wait('gpsimd', 'V')
            p.dma('gpsimd', 'GD',
                  out[t * T:(t + 1) * T, :].rearrange("(s p) d -> p s d", p=P), xb[:])

        ffn_call(xsb[0])
        if stop in ('dbgx', 'dbgg'):
            return
        if stop == 'ffn1t0':
            store_tile(0, xsb[0])
            p.wait('gpsimd', 'GD')
            return
        if stop == 'ffn1b2':
            ffn_call(xsb[1])
            store_tile(0, xsb[0])
            store_tile(1, xsb[1])
            p.wait('gpsimd', 'GD')
            return
        ffn_call(xsb[1])
        if stop == 'ffn1both':
            store_tile(0, xsb[0])
            store_tile(1, xsb[1])
            p.wait('gpsimd', 'GD')
            return
        attn1()
        if stop in ('dbga', 'dbgb'):
            return
        if stop == 'attn1':
            store_tile(0, xsb[0])
            store_tile(1, xsb[1])
            p.wait('gpsimd', 'GD')
            return
        ffn_call(xsb[0])
        ffn_call(xsb[1])
        store_tile(0, xsb[0])
        store_tile(1, xsb[1])
        if stop == 'ffn2':
            p.wait('gpsimd', 'GD')
            return
        for t in range(2, NT):
            bi = t % 2
            p.wait('gpsimd', 'GD')
            p.dma('gpsimd', 'GD', xsb[bi][:],
                  x_in[t * T:(t + 1) * T, :].rearrange("(s p) d -> p s d", p=P))
            ffn_call(xsb[bi])
            if stop != 'tile2nf':
                attn2(xsb[bi])
            if stop in ('a2ln', 'a2q', 'a2s', 'a2pv', 'a2s0', 'a2s1', 'a2s2', 'a2s3'):
                store_tile(t, xsb[bi])
                p.wait('gpsimd', 'GD')
                return
            ffn_call(xsb[bi])
            store_tile(t, xsb[bi])
            if stop in ('tile2', 'tile2nf') and t == 2:
                p.wait('gpsimd', 'GD')
                return
        p.wait('gpsimd', 'GD')

    walks = {}

    def run_walk(ek, eng):
        w = Walk(nc, ek, eng, sems)
        program(w)
        walks[ek] = w

    block.sync(lambda eng: run_walk('sync', eng))
    block.gpsimd(lambda eng: run_walk('gpsimd', eng))
    block.tensor(lambda eng: run_walk('tensor', eng))
    block.vector(lambda eng: run_walk('vector', eng))
    block.scalar(lambda eng: run_walk('scalar', eng))

    cnts = [w.cnt for w in walks.values()]
    for c in cnts[1:]:
        assert c == cnts[0], f"walk divergence: {cnts[0]} vs {c}"

    ctx.close()
    return nc


_NC_CACHE = {}


def kernel(L0_attn_hidden_states, L0_attn_Wq, L0_attn_Wk, L0_attn_Wv, L0_attn_Wo,
           L0_attn_W1, L0_attn_b1, L0_attn_W2, L0_attn_b2,
           L0_ffn_w_gate, L0_ffn_w_up, L0_ffn_w_down):
    hs = np.ascontiguousarray(np.asarray(L0_attn_hidden_states, dtype=np.float32))
    wmap = {k: np.ascontiguousarray(np.asarray(v, dtype=np.float32)) for k, v in [
        ("wq", L0_attn_Wq), ("wk", L0_attn_Wk), ("wv", L0_attn_Wv), ("wo", L0_attn_Wo),
        ("w1", L0_attn_W1), ("b1", L0_attn_b1), ("w2", L0_attn_W2), ("b2", L0_attn_b2),
        ("wg", L0_ffn_w_gate), ("wu", L0_ffn_w_up), ("wd", L0_ffn_w_down)]}
    if 'nc' not in _NC_CACHE:
        _NC_CACHE['nc'] = build_program()
    nc = _NC_CACHE['nc']
    in_maps = [dict(wmap, x=hs[c]) for c in range(8)]
    res = run_bass_kernel_spmd(nc, in_maps, core_ids=list(range(8)))
    return np.stack([r["out"] for r in res.results], axis=0)


# version 05 (final)
# speedup vs baseline: 1.0817x; 1.0443x over previous
import numpy as np
import ml_dtypes
import concourse.bass as bass
import concourse.mybir as mybir
from concourse.bass_utils import run_bass_kernel_spmd

F32 = mybir.dt.float32
BF16 = mybir.dt.bfloat16
AF = mybir.ActivationFunctionType
ALU = mybir.AluOpType
AX = mybir.AxisListType

P = 128
D = 1024
S = 4096
T = 512
NS = T // P
NT = S // T
INNER = 4096
NH = 16
HD = 64
CTX = 1024
POOL = 256
C_RNE = float(np.float32(3 * 2.0 ** 22))
EPS = 1e-5

SBUF_BASE = 16512
OFF_WS = SBUF_BASE
OFF_GU = SBUF_BASE + 65536
OFF_Q1 = SBUF_BASE + 131072
OFF_XQT = SBUF_BASE + 147456
OFF_XQS = SBUF_BASE + 155648
OFF_X0 = SBUF_BASE + 157696
OFF_X1 = SBUF_BASE + 174080
OFF_KCT = SBUF_BASE + 190464
OFF_VC = SBUF_BASE + 194560
OFF_STAT = SBUF_BASE + 198656
OFF_CONST = SBUF_BASE + 199680


def carve(nc, name, shape, dtype, off):
    h = nc.virtual_sbuf_tensor(name, shape, dtype)
    nc._tensor(h.name, list(h.shape), h.dtype, type="SB")
    mloc = nc.lookup_mloc(h)
    mloc.allocated = not nc.target_bir_lowering
    mloc.addr = off
    return h


class Walk:

    def __init__(self, nc, ek, eng, sems):
        self.nc = nc
        self.ek = ek
        self.eng = eng
        self.sems = sems
        self.cnt = {k: 0 for k in sems}

    def inc(self, ch, instr, n=1):
        self.cnt[ch] += n
        if instr is not None:
            instr.then_inc(self.sems[ch], n)

    def dma(self, ek, ch, out, in_, **kw):
        self.cnt[ch] += 16
        if self.ek == ek:
            self.eng.dma_start(out=out, in_=in_, **kw).then_inc(self.sems[ch], 16)

    def wait(self, ek, ch, val=None):
        if self.ek == ek:
            self.eng.wait_ge(self.sems[ch], self.cnt[ch] if val is None else val)

    def on(self, ek):
        return self.ek == ek

    def drain(self, ek):
        if self.ek == ek:
            self.eng.drain()


def build_program(stop=None):
    nc = bass.Bass("TRN2", target_bir_lowering=False)

    x_in = nc.dram_tensor("x", [S, D], F32, kind="ExternalInput").ap()
    wq_in = nc.dram_tensor("wq", [D, D], F32, kind="ExternalInput").ap()
    wk_in = nc.dram_tensor("wk", [D, D], F32, kind="ExternalInput").ap()
    wv_in = nc.dram_tensor("wv", [D, D], F32, kind="ExternalInput").ap()
    wo_in = nc.dram_tensor("wo", [D, D], F32, kind="ExternalInput").ap()
    w1_in = nc.dram_tensor("w1", [D, 512], F32, kind="ExternalInput").ap()
    b1_in = nc.dram_tensor("b1", [512], F32, kind="ExternalInput").ap()
    w2_in = nc.dram_tensor("w2", [512, D], F32, kind="ExternalInput").ap()
    b2_in = nc.dram_tensor("b2", [D], F32, kind="ExternalInput").ap()
    wg_in = nc.dram_tensor("wg", [INNER, D], F32, kind="ExternalInput").ap()
    wu_in = nc.dram_tensor("wu", [INNER, D], F32, kind="ExternalInput").ap()
    wd_in = nc.dram_tensor("wd", [D, INNER], F32, kind="ExternalInput").ap()
    out = nc.dram_tensor("out", [S, D], F32, kind="ExternalOutput").ap()

    s_wgT = nc.dram_tensor("s_wgT", [D, INNER], BF16, kind="Internal").ap()
    s_wuT = nc.dram_tensor("s_wuT", [D, INNER], BF16, kind="Internal").ap()
    s_wdT = nc.dram_tensor("s_wdT", [INNER, D], BF16, kind="Internal").ap()
    s_att = {}
    for k, rows, cols in [("wq", D, D), ("wk", D, D), ("wv", D, D), ("wo", D, D),
                          ("w1", D, 512), ("w2", 512, D)]:
        s_att[k] = nc.dram_tensor("s_" + k, [rows, cols], BF16, kind="Internal").ap()

    identb_t = nc.inline_tensor(np.eye(P, dtype=ml_dtypes.bfloat16), name="c_identb")
    identf_t = nc.inline_tensor(np.eye(P, dtype=np.float32), name="c_identf")
    maskc = np.where(np.arange(P)[None, :] <= np.arange(P)[:, None],
                     np.float32(0.0), np.float32(-1e30)).astype(np.float32)
    mask_t = nc.inline_tensor(maskc, name="c_mask")
    ones_t = nc.inline_tensor(np.ones((1, P), dtype=np.float32), name="c_ones")
    cvals = np.broadcast_to(np.array([[C_RNE, 0.125]], dtype=np.float32), (P, 2)).copy()
    cval_t = nc.inline_tensor(cvals, name="c_cvals")

    ws = carve(nc, "ws", [P, 4, 8, 1024], BF16, OFF_WS)
    gu = carve(nc, "gu", [P, NS, INNER], F32, OFF_GU)
    guq = [carve(nc, f"guq{s}", [P, INNER], BF16, OFF_GU + s * 16384) for s in range(NS)]
    q1 = carve(nc, "q1", [P, 4096], F32, OFF_Q1)
    gqT = [carve(nc, f"gqT{i}", [P, 32, P], BF16, OFF_Q1 + i * 8192) for i in range(2)]
    xqT = carve(nc, "xqT", [P, 8, T], BF16, OFF_XQT)
    xqs = carve(nc, "xqs", [P, D], BF16, OFF_XQS)
    xsb = [carve(nc, "x0", [P, NS, D], F32, OFF_X0),
           carve(nc, "x1", [P, NS, D], F32, OFF_X1)]
    kct = carve(nc, "kct", [P, 8, POOL], BF16, OFF_KCT)
    vc = carve(nc, "vc", [P, 2, D], BF16, OFF_VC)
    st = carve(nc, "st", [P, 256], F32, OFF_STAT)
    identb = carve(nc, "identb", [P, P], BF16, OFF_CONST)
    identf = carve(nc, "identf", [P, P], F32, OFF_CONST + 256)
    maskb = carve(nc, "maskb", [P, P], F32, OFF_CONST + 768)
    onesb = carve(nc, "onesb", [1, P], F32, OFF_CONST + 1280)
    b1b = carve(nc, "b1b", [P, 4], F32, OFF_CONST + 1792)
    b2b = carve(nc, "b2b", [P, 8], F32, OFF_CONST + 1808)
    cvalb = carve(nc, "cvalb", [P, 2], F32, OFF_CONST + 1840)
    lnT2 = carve(nc, "lnT2", [P, 8, T], BF16, OFF_GU + 0)
    qt2 = carve(nc, "qt2", [P, 8, T], BF16, OFF_GU + 8192)
    ot2 = carve(nc, "ot2", [P, 8, T], BF16, OFF_GU + 16384)
    pr2 = carve(nc, "pr2", [P, NH, POOL], BF16, OFF_GU + 24576)
    prT2 = carve(nc, "prT2", [P, 2, NH, P], BF16, OFF_GU + 32768)
    lns = carve(nc, "lns", [P, D], BF16, OFF_GU + 40960)
    lnT1 = carve(nc, "lnT1", [P, 8, CTX], BF16, OFF_GU + 0)
    qt1 = carve(nc, "qt1", [P, 8, CTX], BF16, OFF_GU + 16384)
    kt1 = carve(nc, "kt1", [P, 8, CTX], BF16, OFF_GU + 32768)
    v1 = carve(nc, "v1", [P, 8, D], BF16, OFF_GU + 49152)
    ot1 = carve(nc, "ot1", [P, 8, CTX], BF16, OFF_Q1 + 0)
    relu1 = carve(nc, "relu1", [P, 4, CTX], BF16, OFF_XQT)
    pool1 = carve(nc, "pool1", [P, 4, POOL], BF16, OFF_XQS)
    compT = carve(nc, "compT", [P, 8, POOL], BF16, OFF_XQT)
    pr1 = [carve(nc, f"pr1_{i}", [P, CTX], BF16, OFF_XQT + i * 2048) for i in range(2)]
    prT1 = [carve(nc, f"prT1_{i}", [P, 8, P], BF16, OFF_XQT + 4096 + i * 2048)
            for i in range(2)]
    wfg = carve(nc, "wfg", [P, 32, D], F32, OFF_WS)
    wfd = carve(nc, "wfd", [P, 8, INNER], F32, OFF_WS)
    wqj = carve(nc, "wqj", [P, 4096], BF16, OFF_KCT)
    wcol = carve(nc, "wcol", [P, 8, 1024], BF16, OFF_Q1 + 4096)
    wdcol = carve(nc, "wdcol", [P, 32, P], BF16, OFF_XQT)

    nc.const_aps.register(F32, C_RNE, cvalb[:, 0:1])
    nc.const_aps.register(F32, 0.125, cvalb[:, 1:2])

    COL = dict(sums=0, sumsq=4, mx=8, mn=12, mean=16, m2=20, var=24, rstd=28,
               amax=32, sx=36, isx=40, qsc=44, qb=48, gsc=52, usc=56,
               gamax=60, gsx=64, gisx=68, dsc=72, tmp=76,
               negmax=80, sume=96, rsum=112,
               wtmp=128, swrep=224, mwg=225, mwu=226, mwd=227)

    def stc(name, s=0):
        c = COL[name] + s
        return st[:, c:c + 1]

    def stv(name, w, s=0):
        c = COL[name] + s
        return st[:, c:c + w]

    from contextlib import ExitStack
    ctx = ExitStack()
    sems = {k: ctx.enter_context(nc.semaphore(k)) for k in ['SD', 'GD', 'PE', 'V', 'A']}
    psa = ctx.enter_context(nc.psum_tensor("psa", [P, 2048], F32))
    psb = ctx.enter_context(nc.psum_tensor("psb", [P, 1024], F32))
    psc = ctx.enter_context(nc.psum_tensor("psc", [P, 2048], BF16))
    block = ctx.enter_context(nc.Block())

    def program(p: Walk):
        slot_rel = {0: 0, 1: 0, 2: 0, 3: 0}

        def mm_accum(p_, out_ap, ops, inc=True):
            last = None
            if p_.on('tensor'):
                n = len(ops)
                for i, (l, r) in enumerate(ops):
                    last = p_.eng.matmul(out_ap, l, r, start=(i == 0), stop=(i == n - 1))
            if inc:
                p_.inc('PE', last)
            return last

        def transpose_batch(p_, pairs, inc=True):
            last = None
            if p_.on('tensor'):
                for o, i in pairs:
                    last = p_.eng.transpose(o, i, identb[:])
            if inc:
                p_.inc('PE', last)
            return last

        def pe_phase_barrier():
            p.wait('tensor', 'V')
            p.wait('tensor', 'A')

        p.dma('sync', 'SD', identb[:], identb_t.ap())
        p.dma('sync', 'SD', identf[:], identf_t.ap())
        p.dma('sync', 'SD', maskb[:], mask_t.ap())
        p.dma('sync', 'SD', onesb[:], ones_t.ap())
        p.dma('sync', 'SD', cvalb[:], cval_t.ap())
        p.dma('sync', 'SD', b1b[:], b1_in.rearrange("(c p) -> p c", p=P),
              allow_slow_non_contiguous=True)
        p.dma('sync', 'SD', b2b[:], b2_in.rearrange("(c p) -> p c", p=P),
              allow_slow_non_contiguous=True)
        p.dma('gpsimd', 'GD', xsb[0][:], x_in[0:T, :].rearrange("(s p) d -> p s d", p=P))
        p.dma('gpsimd', 'GD', xsb[1][:], x_in[T:2 * T, :].rearrange("(s p) d -> p s d", p=P))

        def quant_weight(w_src, n_rows, scr_T, mw_col):
            rl = D if n_rows == 32 else INNER
            wf = wfg if n_rows == 32 else wfd
            nelem = float(n_rows * P * rl)
            p.wait('sync', 'A')
            p.dma('sync', 'SD', wf[:], w_src.rearrange("(j p) i -> p j i", p=P))
            p.wait('scalar', 'SD')
            if p.on('scalar'):
                for j in range(n_rows):
                    ins = p.eng.activation(out=q1[:, 0:rl], in_=wf[:, j, :], func=AF.Abs,
                                           accum_out=stv('wtmp', 1, j))
                ins.then_inc(p.sems['A'], 1)
            p.cnt['A'] += 1
            p.wait('vector', 'A')
            if p.on('vector'):
                ins = p.eng.reduce_sum(out=stv('wtmp', 1, 64), in_=stv('wtmp', n_rows, 0),
                                       axis=AX.X)
                ins.then_inc(p.sems['V'], 1)
                p.eng.drain()
            p.cnt['V'] += 1
            p.wait('tensor', 'V')
            if p.on('tensor'):
                ins = p.eng.transpose(psa[0:1, 0:P], stv('wtmp', 1, 64), identf[:])
            p.inc('PE', ins if p.on('tensor') else None)
            p.wait('vector', 'PE')
            if p.on('vector'):
                e = p.eng
                e.tensor_copy(out=st[0:1, 0:P], in_=psa[0:1, 0:P])
                e.drain()
                c65 = st[0:1, COL['wtmp'] + 65:COL['wtmp'] + 66]
                c66 = st[0:1, COL['wtmp'] + 66:COL['wtmp'] + 67]
                c67 = st[0:1, COL['wtmp'] + 67:COL['wtmp'] + 68]
                e.reduce_sum(out=c65, in_=st[0:1, 0:P], axis=AX.X)
                e.drain()
                e.tensor_scalar(out=c65, in0=c65, scalar1=1.0 / nelem, scalar2=1e-5,
                                op0=ALU.mult, op1=ALU.max)
                e.drain()
                e.reciprocal(out=c66, in_=c65)
                e.drain()
                ins = e.reciprocal(out=c67, in_=c66)
                ins.then_inc(p.sems['V'], 1)
                e.drain()
            p.cnt['V'] += 1
            p.wait('tensor', 'V')
            if p.on('tensor'):
                ins = p.eng.matmul(psa[:, 256:258], onesb[0:1, 0:P],
                                   st[0:1, COL['wtmp'] + 66:COL['wtmp'] + 68],
                                   start=True, stop=True)
            p.inc('PE', ins if p.on('tensor') else None)
            p.wait('vector', 'PE')
            if p.on('vector'):
                p.eng.tensor_copy(out=stc('swrep'), in_=psa[:, 256:257])
                ins = p.eng.tensor_copy(out=st[:, mw_col:mw_col + 1], in_=psa[:, 257:258])
                p.eng.drain()
                ins.then_inc(p.sems['V'], 1)
            p.cnt['V'] += 1

            for j in range(n_rows):
                p.wait('scalar', 'V')
                if p.on('scalar'):
                    ins = p.eng.activation(out=q1[:, 0:rl], in_=wf[:, j, :],
                                           func=AF.Identity, scale=stc('swrep'),
                                           bias=C_RNE)
                    ins.then_inc(p.sems['A'], 1)
                p.cnt['A'] += 1
                p.wait('vector', 'A')
                p.wait('vector', 'PE')
                if n_rows == 32 and j % 8 == 0:
                    p.wait('vector', 'GD')
                if p.on('vector'):
                    p.eng.tensor_scalar(out=q1[:, 0:rl], in0=q1[:, 0:rl], scalar1=C_RNE,
                                        scalar2=1.0, op0=ALU.subtract, op1=ALU.min)
                    p.eng.drain()
                    ins = p.eng.tensor_scalar(out=wqj[:, 0:rl], in0=q1[:, 0:rl],
                                              scalar1=-1.0, scalar2=None, op0=ALU.max)
                    ins.then_inc(p.sems['V'], 1)
                p.cnt['V'] += 1
                ntp = rl // P
                p.wait('tensor', 'V')
                for b in range(ntp // 8):
                    ph = b % 2
                    p.wait('tensor', 'V')
                    pairs = [(psc[:, ph * 1024 + k * P:ph * 1024 + (k + 1) * P],
                              wqj[:, (b * 8 + k) * P:(b * 8 + k + 1) * P])
                             for k in range(8)]
                    transpose_batch(p, pairs)
                    p.wait('vector', 'PE')
                    if n_rows == 8 and b == 0:
                        p.wait('vector', 'GD')
                    if p.on('vector'):
                        src = psc[:, ph * 1024:(ph + 1) * 1024].rearrange(
                            "p (a b) -> p a b", b=P)
                        if n_rows == 32:
                            ins = p.eng.tensor_copy(
                                out=wcol[:, :, (j % 8) * P:(j % 8 + 1) * P], in_=src)
                        else:
                            ins = p.eng.tensor_copy(
                                out=wdcol[:, b * 8:(b + 1) * 8, :], in_=src)
                        ins.then_inc(p.sems['V'], 1)
                    p.cnt['V'] += 1
                if n_rows == 32 and j % 8 == 7:
                    jg = j // 8
                    p.wait('gpsimd', 'V')
                    p.dma('gpsimd', 'GD',
                          scr_T.rearrange("(ic p) o -> p ic o", p=P)[:, :, jg * 1024:(jg + 1) * 1024],
                          wcol[:])
                elif n_rows == 8:
                    p.wait('gpsimd', 'V')
                    p.dma('gpsimd', 'GD',
                          scr_T.rearrange("(ic p) o -> p ic o", p=P)[:, :, j * P:(j + 1) * P],
                          wdcol[:])

        quant_weight(wg_in, 32, s_wgT, COL['mwg'])
        if stop == 'dbgq':
            p.wait('sync', 'GD')
            p.dma('sync', 'SD', out[0:1024, 0:1024],
                  s_wgT.bitcast(F32)[:, 0:1024])
            p.wait('sync', 'V')
            p.wait('sync', 'A')
            p.dma('sync', 'SD', out[1024:1152, 0:256], st[:])
            p.wait('sync', 'SD')
            return
        quant_weight(wu_in, 32, s_wuT, COL['mwu'])
        quant_weight(wd_in, 8, s_wdT, COL['mwd'])

        p.wait('gpsimd', 'A')
        p.wait('gpsimd', 'V')
        p.wait('gpsimd', 'PE')
        ws_flat = ws.rearrange("p a b c -> p (a b) c")
        for key, src, rows, cols in [("wq", wq_in, D, D), ("wk", wk_in, D, D),
                                     ("wv", wv_in, D, D), ("wo", wo_in, D, D),
                                     ("w1", w1_in, D, 512), ("w2", w2_in, 512, D)]:
            p.wait('gpsimd', 'GD')
            p.dma('gpsimd', 'GD', ws_flat[:, 0:rows // P, 0:cols],
                  src.rearrange("(j p) o -> p j o", p=P))
            p.wait('gpsimd', 'GD')
            p.dma('gpsimd', 'GD', s_att[key].rearrange("(j p) o -> p j o", p=P),
                  ws_flat[:, 0:rows // P, 0:cols])
        cast_done_gd = p.cnt['GD']


        def ln_quant_stats(xb):
            p.wait('vector', 'GD')
            p.wait('scalar', 'GD')
            p.wait('scalar', 'PE')
            p.wait('scalar', 'V')
            p.drain('vector')
            if p.on('vector'):
                for s in range(NS):
                    p.eng.reduce_sum(out=stc('sums', s), in_=xb[:, s, :], axis=AX.X)
                    p.eng.reduce_max(out=stc('mx', s), in_=xb[:, s, :], axis=AX.X)
                    ins = p.eng.tensor_reduce(out=stc('mn', s), in_=xb[:, s, :],
                                              axis=AX.X, op=ALU.min)
                ins.then_inc(p.sems['V'], 1)
            p.cnt['V'] += 1
            if p.on('scalar'):
                for s in range(NS):
                    ins = p.eng.activation(out=q1[:, 0:D], in_=xb[:, s, :],
                                           func=AF.Square, accum_out=stc('sumsq', s))
                ins.then_inc(p.sems['A'], 1)
            p.cnt['A'] += 1
            p.wait('vector', 'A')
            if p.on('vector'):
                e = p.eng
                e.drain()
                e.tensor_scalar(out=stv('mean', NS), in0=stv('sums', NS),
                                scalar1=1.0 / D, scalar2=None, op0=ALU.mult)
                e.drain()
                e.tensor_mul(out=stv('m2', NS), in0=stv('mean', NS), in1=stv('mean', NS))
                e.drain()
                e.scalar_tensor_tensor(out=stv('var', NS), in0=stv('sumsq', NS),
                                       scalar=1.0 / D, in1=stv('m2', NS),
                                       op0=ALU.mult, op1=ALU.subtract)
                e.drain()
                e.tensor_scalar(out=stv('var', NS), in0=stv('var', NS), scalar1=EPS,
                                scalar2=None, op0=ALU.add)
                e.drain()
                ins = e.reciprocal(out=stv('rstd', NS), in_=stv('var', NS))
                ins.then_inc(p.sems['V'], 1)
            p.cnt['V'] += 1
            p.wait('scalar', 'V')
            if p.on('scalar'):
                ins = p.eng.activation(out=stv('rstd', NS), in_=stv('rstd', NS),
                                       func=AF.Sqrt)
                ins.then_inc(p.sems['A'], 1)
            p.cnt['A'] += 1
            p.wait('vector', 'A')
            if p.on('vector'):
                e = p.eng
                e.tensor_sub(out=stv('amax', NS), in0=stv('mx', NS), in1=stv('mean', NS))
                e.tensor_sub(out=stv('tmp', NS), in0=stv('mean', NS), in1=stv('mn', NS))
                e.drain()
                e.tensor_max(out=stv('amax', NS), in0=stv('amax', NS), in1=stv('tmp', NS))
                e.drain()
                e.tensor_mul(out=stv('amax', NS), in0=stv('amax', NS), in1=stv('rstd', NS))
                e.drain()
                e.tensor_scalar(out=stv('amax', NS), in0=stv('amax', NS), scalar1=1e-5,
                                scalar2=None, op0=ALU.max)
                e.drain()
                e.reciprocal(out=stv('sx', NS), in_=stv('amax', NS))
                e.drain()
                e.tensor_scalar(out=stv('sx', NS), in0=stv('sx', NS), scalar1=127.0,
                                scalar2=None, op0=ALU.mult)
                e.drain()
                e.reciprocal(out=stv('isx', NS), in_=stv('sx', NS))
                e.drain()
                e.tensor_mul(out=stv('qsc', NS), in0=stv('rstd', NS), in1=stv('sx', NS))
                e.drain()
                e.tensor_mul(out=stv('qb', NS), in0=stv('mean', NS), in1=stv('qsc', NS))
                e.drain()
                e.tensor_scalar(out=stv('qb', NS), in0=stv('qb', NS), scalar1=-1.0,
                                scalar2=None, op0=ALU.mult)
                e.drain()
                e.tensor_scalar(out=stv('gsc', NS), in0=stv('isx', NS),
                                scalar1=stc('mwg'), scalar2=None, op0=ALU.mult)
                e.drain()
                ins = e.tensor_scalar(out=stv('usc', NS), in0=stv('isx', NS),
                                      scalar1=stc('mwu'), scalar2=None, op0=ALU.mult)
                p.eng.drain()
                ins.then_inc(p.sems['V'], 1)
            p.cnt['V'] += 1

        def ffn_call(xb):
            psc_rel = {0: 0, 1: 0}
            guq_v = {}
            ln_quant_stats(xb)
            for s in range(NS):
                p.wait('scalar', 'V')
                if p.on('scalar'):
                    ins = p.eng.activation(out=q1[:, (s % 2) * D:(s % 2 + 1) * D],
                                           in_=xb[:, s, :], func=AF.Identity,
                                           scale=stc('qsc', s), bias=stc('qb', s))
                    ins.then_inc(p.sems['A'], 1)
                p.cnt['A'] += 1
                p.wait('vector', 'A')
                p.wait('vector', 'PE')
                if p.on('vector'):
                    ins = p.eng.tensor_scalar(out=xqs[:, 0:D],
                                              in0=q1[:, (s % 2) * D:(s % 2 + 1) * D],
                                              scalar1=C_RNE, scalar2=C_RNE,
                                              op0=ALU.add, op1=ALU.subtract)
                    ins.then_inc(p.sems['V'], 1)
                p.cnt['V'] += 1
                xqs_v = p.cnt['V']
                ph = s % 2
                p.wait('tensor', 'V', xqs_v)
                p.wait('tensor', 'V', psc_rel[ph])
                pairs = [(psc[:, ph * 1024 + k * P:ph * 1024 + (k + 1) * P],
                          xqs[:, k * P:(k + 1) * P]) for k in range(8)]
                transpose_batch(p, pairs)
                p.wait('vector', 'PE')
                if p.on('vector'):
                    ins = p.eng.tensor_copy(
                        out=xqT[:, :, s * P:(s + 1) * P],
                        in_=psc[:, ph * 1024:(ph + 1) * 1024]
                            .rearrange("p (a b) -> p a b", b=P))
                    ins.then_inc(p.sems['V'], 1)
                p.cnt['V'] += 1
                psc_rel[ph] = p.cnt['V']

            if stop == 'dbgx':
                p.wait('sync', 'V')
                p.wait('sync', 'A')
                p.dma('sync', 'SD', out[0:128, 0:512],
                      xqT.rearrange("p a b -> p (a b)").bitcast(F32)[:, 0:512])
                p.dma('sync', 'SD', out[1024:1152, 0:256], st[:])
                p.dma('sync', 'SD', out[1280:1408, 0:1024], q1[:, 1024:2048])
                p.dma('sync', 'SD', out[1536:1664, 0:512],
                      xqs.bitcast(F32)[:, 0:512])
                p.wait('sync', 'SD')
                return

            def mm_phase(scr, dest_of, evac_eng):
                ev = {'n': 0}
                ch = 'V' if evac_eng == 'vector' else 'A'
                p.wait('tensor', 'V')
                p.wait('tensor', 'A')
                for oq in range(4):
                    p.wait('sync', 'PE', slot_rel[oq])
                    p.wait('sync', 'GD')
                    p.dma('sync', 'SD', ws[:, oq, :, :],
                          scr.rearrange("(ic p) o -> p ic o", p=P)
                          [:, :, oq * 1024:(oq + 1) * 1024])
                    p.wait('tensor', 'SD')
                    for s in range(NS):
                        for oh in range(2):
                            g = ev['n']
                            half = g % 2
                            if g >= 2:
                                p.wait('tensor', ch, ev[g - 2])
                            ops = [(xqT[:, ic, s * P:(s + 1) * P],
                                    ws[:, oq, ic, oh * 512:(oh + 1) * 512])
                                   for ic in range(8)]
                            mm_accum(p, psb[:, half * 512:(half + 1) * 512], ops)
                            p.wait(evac_eng, 'PE')
                            if p.on(evac_eng):
                                ins = dest_of(p.eng, s, oq, oh, half)
                                ins.then_inc(p.sems[ch], 1)
                            p.cnt[ch] += 1
                            ev[g] = p.cnt[ch]
                            ev['n'] += 1
                    slot_rel[oq] = p.cnt['PE']

            def gate_dest(e, s, oq, oh, half):
                sl = gu[:, s, oq * 1024 + oh * 512:oq * 1024 + (oh + 1) * 512]
                return e.activation(out=sl, in_=psb[:, half * 512:(half + 1) * 512],
                                    func=AF.Silu, scale=stc('gsc', s))

            def up_dest(e, s, oq, oh, half):
                sl = gu[:, s, oq * 1024 + oh * 512:oq * 1024 + (oh + 1) * 512]
                return e.scalar_tensor_tensor(out=sl,
                                              in0=psb[:, half * 512:(half + 1) * 512],
                                              scalar=stc('usc', s), in1=sl,
                                              op0=ALU.mult, op1=ALU.mult)

            mm_phase(s_wgT, gate_dest, 'scalar')
            mm_phase(s_wuT, up_dest, 'vector')

            p.wait('vector', 'V')
            p.drain('vector')
            if p.on('vector'):
                for s in range(NS):
                    ins = p.eng.reduce_max(out=stc('gamax', s), in_=gu[:, s, :],
                                           axis=AX.X, apply_absolute_value=True)
                ins.then_inc(p.sems['V'], 1)
            p.cnt['V'] += 1
            p.drain('vector')
            if p.on('vector'):
                e = p.eng
                e.drain()
                e.tensor_scalar(out=stv('gamax', NS), in0=stv('gamax', NS), scalar1=1e-5,
                                scalar2=None, op0=ALU.max)
                e.drain()
                e.reciprocal(out=stv('gsx', NS), in_=stv('gamax', NS))
                e.drain()
                e.tensor_scalar(out=stv('gsx', NS), in0=stv('gsx', NS), scalar1=127.0,
                                scalar2=None, op0=ALU.mult)
                e.drain()
                e.reciprocal(out=stv('gisx', NS), in_=stv('gsx', NS))
                e.drain()
                e.tensor_scalar(out=stv('dsc', NS), in0=stv('gisx', NS),
                                scalar1=stc('mwd'), scalar2=None, op0=ALU.mult)
                e.drain()
                ins = e.tensor_scalar(out=stv('dsc', NS), in0=stv('dsc', NS),
                                      scalar1=0.5, scalar2=None, op0=ALU.mult)
                e.drain()
                ins.then_inc(p.sems['V'], 1)
            p.cnt['V'] += 1
            for s in range(NS):
                p.wait('scalar', 'V')
                if p.on('scalar'):
                    ins = p.eng.activation(out=q1[:, 0:INNER], in_=gu[:, s, :],
                                           func=AF.Identity, scale=stc('gsx', s),
                                           bias=C_RNE)
                    ins.then_inc(p.sems['A'], 1)
                p.cnt['A'] += 1
                p.wait('vector', 'A')
                if p.on('vector'):
                    ins = p.eng.tensor_scalar(out=guq[s][:], in0=q1[:, 0:INNER],
                                              scalar1=C_RNE, scalar2=None,
                                              op0=ALU.subtract)
                    ins.then_inc(p.sems['V'], 1)
                p.cnt['V'] += 1
                guq_v[s] = p.cnt['V']

            if stop == 'dbgg':
                p.wait('sync', 'V')
                p.wait('sync', 'A')
                p.dma('sync', 'SD', out[0:128, 0:1024], gu[:, 0, 0:1024])
                p.dma('sync', 'SD', out[128:256, 0:1024],
                      guq[0].bitcast(F32)[:, 0:1024])
                p.dma('sync', 'SD', out[1024:1152, 0:256], st[:])
                p.wait('sync', 'SD')
                return

            for sl in range(4):
                p.wait('sync', 'PE', slot_rel[sl])
                p.dma('sync', 'SD', ws[:, sl, :, :],
                      s_wdT.rearrange("(ic p) o -> p ic o", p=P)[:, sl * 8:(sl + 1) * 8, :])
            p.wait('tensor', 'SD')
            dn = {'n': 0}
            for s in range(NS):
                for b in range(4):
                    ph = b % 2
                    p.wait('tensor', 'V', guq_v[s])
                    p.wait('tensor', 'V', psc_rel[ph])
                    pairs = [(psc[:, ph * 1024 + k * P:ph * 1024 + (k + 1) * P],
                              guq[s][:, (b * 8 + k) * P:(b * 8 + k + 1) * P])
                             for k in range(8)]
                    transpose_batch(p, pairs)
                    p.wait('vector', 'PE')
                    if p.on('vector'):
                        ins = p.eng.tensor_copy(
                            out=gqT[s % 2][:, b * 8:(b + 1) * 8, :],
                            in_=psc[:, ph * 1024:(ph + 1) * 1024]
                                .rearrange("p (a b) -> p a b", b=P))
                        ins.then_inc(p.sems['V'], 1)
                    p.cnt['V'] += 1
                    psc_rel[ph] = p.cnt['V']
                p.wait('tensor', 'V')
                for oh in range(2):
                    g = dn['n']
                    half = g % 2
                    if g >= 2:
                        p.wait('tensor', 'V', dn[g - 2])
                    ops = [(gqT[s % 2][:, ic, :],
                            ws[:, ic // 8, ic % 8, oh * 512:(oh + 1) * 512])
                           for ic in range(32)]
                    mm_accum(p, psb[:, half * 512:(half + 1) * 512], ops)
                    p.wait('vector', 'PE')
                    if p.on('vector'):
                        ins = p.eng.scalar_tensor_tensor(
                            out=xb[:, s, oh * 512:(oh + 1) * 512],
                            in0=psb[:, half * 512:(half + 1) * 512],
                            scalar=stc('dsc', s),
                            in1=xb[:, s, oh * 512:(oh + 1) * 512],
                            op0=ALU.mult, op1=ALU.add)
                        ins.then_inc(p.sems['V'], 1)
                    p.cnt['V'] += 1
                    dn[g] = p.cnt['V']
                    dn['n'] += 1
            for sl in range(4):
                slot_rel[sl] = p.cnt['PE']

        def ln_to_lnT(xb, lnT, toff_sub):
            ln_quant_stats(xb)
            for s in range(NS):
                p.wait('vector', 'V')
                if p.on('vector'):
                    ins = p.eng.tensor_scalar(out=lns[:, 0:D], in0=xb[:, s, :],
                                              scalar1=stc('mean', s),
                                              scalar2=stc('rstd', s),
                                              op0=ALU.subtract, op1=ALU.mult)
                    ins.then_inc(p.sems['V'], 1)
                p.cnt['V'] += 1
                ph = s % 2
                p.wait('tensor', 'V')
                pairs = [(psc[:, ph * 1024 + k * P:ph * 1024 + (k + 1) * P],
                          lns[:, k * P:(k + 1) * P]) for k in range(8)]
                transpose_batch(p, pairs)
                p.wait('vector', 'PE')
                if p.on('vector'):
                    ins = p.eng.tensor_copy(
                        out=lnT[:, :, (toff_sub + s) * P:(toff_sub + s + 1) * P],
                        in_=psc[:, ph * 1024:(ph + 1) * 1024]
                            .rearrange("p (a b) -> p a b", b=P))
                    ins.then_inc(p.sems['V'], 1)
                p.cnt['V'] += 1

        def proj_fm(wslot, lnT, dst, ncol, tlen, scale_const=None, evac='scalar'):
            pe_phase_barrier()
            g = {'n': 0}
            ch = 'V' if evac == 'vector' else 'A'
            for oc in range(ncol):
                for th in range(tlen // 512):
                    gg = g['n']
                    half = gg % 2
                    if gg >= 2:
                        p.wait('tensor', ch, g[gg - 2])
                    ops = [(ws[:, wslot, ic, oc * P:(oc + 1) * P],
                            lnT[:, ic, th * 512:(th + 1) * 512]) for ic in range(8)]
                    mm_accum(p, psb[:, half * 512:(half + 1) * 512], ops)
                    p.wait(evac, 'PE')
                    if p.on(evac):
                        if evac == 'scalar':
                            ins = p.eng.activation(
                                out=dst[:, oc, th * 512:(th + 1) * 512],
                                in_=psb[:, half * 512:(half + 1) * 512],
                                func=AF.Copy,
                                scale=scale_const if scale_const is not None else 1.0)
                        else:
                            ins = p.eng.tensor_copy(
                                out=dst[:, oc, th * 512:(th + 1) * 512],
                                in_=psb[:, half * 512:(half + 1) * 512])
                        ins.then_inc(p.sems[ch], 1)
                    p.cnt[ch] += 1
                    g[gg] = p.cnt[ch]
                    g['n'] += 1

        def attn2(xb):
            ln_to_lnT(xb, lnT2, 0)
            if stop == 'a2ln':
                return
            p.wait('sync', 'PE', slot_rel[0])
            p.dma('sync', 'SD', ws[:, 0, :, :],
                  s_att['wq'].rearrange("(ic p) o -> p ic o", p=P))
            p.wait('sync', 'PE', slot_rel[1])
            p.dma('sync', 'SD', ws[:, 1, :, :],
                  s_att['wo'].rearrange("(ic p) o -> p ic o", p=P))
            p.wait('tensor', 'SD')
            proj_fm(0, lnT2, qt2, 8, T, scale_const=0.125, evac='scalar')
            slot_rel[0] = p.cnt['PE']
            if stop == 'a2q':
                return
            a_rel = {0: 0, 1: 0}
            ng4 = 1 if stop in ('a2s0', 'a2s1', 'a2s2') else 4
            for s in range(NS if stop not in ('a2s', 'a2s0', 'a2s1', 'a2s2', 'a2s3') else 1):
                for g4 in range(ng4):
                    par = g4 % 2
                    sc = psa[:, par * 1024:(par + 1) * 1024]
                    if stop != 'a2s0':
                        p.wait('tensor', 'A', a_rel[par])
                    for hh in range(4):
                        h = g4 * 4 + hh
                        po = (h % 2) * 64
                        sl = (hh % 2) * 2 + hh // 2
                        mm_accum(p, sc[:, sl * POOL:(sl + 1) * POOL],
                                 [(qt2[po:po + 64, h // 2, s * P:(s + 1) * P],
                                   kct[po:po + 64, h // 2, 0:POOL])],
                                 inc=(hh == 3))
                    if stop == 'a2s0':
                        return
                    p.wait('vector', 'PE')
                    if p.on('vector'):
                        for hh in range(4):
                            sl = (hh % 2) * 2 + hh // 2
                            ins = p.eng.reduce_max(
                                out=stc('negmax', g4 * 4 + hh),
                                in_=sc[:, sl * POOL:(sl + 1) * POOL],
                                axis=AX.X, negate=True)
                        p.eng.drain()
                        ins.then_inc(p.sems['V'], 1)
                    p.cnt['V'] += 1
                    if stop == 'a2s1':
                        return
                    p.wait('scalar', 'V')
                    p.wait('scalar', 'PE')
                    if p.on('scalar'):
                        for hh in range(4):
                            h = g4 * 4 + hh
                            sl = (hh % 2) * 2 + hh // 2
                            ins = p.eng.activation(out=pr2[:, h, :],
                                                   in_=sc[:, sl * POOL:(sl + 1) * POOL],
                                                   func=AF.Exp, bias=stc('negmax', h),
                                                   accum_out=stc('sume', h))
                        ins.then_inc(p.sems['A'], 1)
                    p.cnt['A'] += 1
                    a_rel[par] = p.cnt['A']
                    if stop == 'a2s2':
                        return
                p.wait('vector', 'A')
                if p.on('vector'):
                    p.eng.reciprocal(out=stv('rsum', NH), in_=stv('sume', NH))
                    p.eng.drain()
                    for h in range(NH):
                        ins = p.eng.tensor_scalar(out=pr2[:, h, :], in0=pr2[:, h, :],
                                                  scalar1=stc('rsum', h), scalar2=None,
                                                  op0=ALU.mult)
                    p.eng.drain()
                    ins.then_inc(p.sems['V'], 1)
                p.cnt['V'] += 1
                for bb in range(2):
                    p.wait('tensor', 'V')
                    pairs = []
                    for hh in range(8):
                        h = bb * 8 + hh
                        for kc in range(2):
                            pairs.append(
                                (psc[:, (hh * 2 + kc) * P:(hh * 2 + kc + 1) * P],
                                 pr2[:, h, kc * P:(kc + 1) * P]))
                    transpose_batch(p, pairs)
                    p.wait('vector', 'PE')
                    if p.on('vector'):
                        ins = p.eng.tensor_copy(
                            out=prT2[:, :, bb * 8:(bb + 1) * 8, :],
                            in_=psc[:, 0:2048].rearrange("p (h k c) -> p k h c",
                                                         k=2, c=P))
                        ins.then_inc(p.sems['V'], 1)
                    p.cnt['V'] += 1
                if stop == 'a2s3':
                    return
                p.wait('tensor', 'V')
                pvv = psb.rearrange("p (a b) -> p a b", b=P)
                lastmm = None
                if p.on('tensor'):
                    for h in range(NH):
                        po = (h % 2) * 64
                        for kc in range(2):
                            lastmm = p.eng.matmul(pvv[po:po + 64, h // 2, :],
                                                  vc[:, kc, h * HD:(h + 1) * HD],
                                                  prT2[:, kc, h, :],
                                                  start=(kc == 0), stop=(kc == 1))
                p.inc('PE', lastmm)
                p.wait('vector', 'PE')
                if p.on('vector'):
                    ins = p.eng.tensor_copy(out=ot2[:, :, s * P:(s + 1) * P], in_=pvv[:])
                    ins.then_inc(p.sems['V'], 1)
                p.cnt['V'] += 1
            if stop in ('a2s', 'a2pv'):
                return
            g = {'n': 0}
            for s in range(NS):
                for oh in range(2):
                    gg = g['n']
                    half = gg % 2
                    p.wait('tensor', 'V', None if gg < 2 else g[gg - 2])
                    ops = [(ot2[:, ic, s * P:(s + 1) * P],
                            ws[:, 1, ic, oh * 512:(oh + 1) * 512]) for ic in range(8)]
                    mm_accum(p, psb[:, half * 512:(half + 1) * 512], ops)
                    p.wait('vector', 'PE')
                    if p.on('vector'):
                        ins = p.eng.tensor_tensor(
                            out=xb[:, s, oh * 512:(oh + 1) * 512],
                            in0=psb[:, half * 512:(half + 1) * 512],
                            in1=xb[:, s, oh * 512:(oh + 1) * 512], op=ALU.add)
                        ins.then_inc(p.sems['V'], 1)
                    p.cnt['V'] += 1
                    g[gg] = p.cnt['V']
                    g['n'] += 1
            slot_rel[1] = p.cnt['PE']

        def attn1():
            ln_to_lnT(xsb[0], lnT1, 0)
            ln_to_lnT(xsb[1], lnT1, 4)
            p.wait('sync', 'PE')
            p.wait('sync', 'GD', cast_done_gd)
            p.dma('sync', 'SD', ws[:, 0, :, 0:512],
                  s_att['w1'].rearrange("(ic p) o -> p ic o", p=P))
            p.dma('sync', 'SD', ws[:, 1, 0:4, :],
                  s_att['w2'].rearrange("(ic p) o -> p ic o", p=P))
            p.wait('tensor', 'SD')
            pe_phase_barrier()
            g = {'n': 0}
            for oc in range(4):
                for th in range(2):
                    gg = g['n']
                    half = gg % 2
                    if gg >= 2:
                        p.wait('tensor', 'A', g[gg - 2])
                    ops = [(ws[:, 0, ic, oc * P:(oc + 1) * P],
                            lnT1[:, ic, th * 512:(th + 1) * 512]) for ic in range(8)]
                    mm_accum(p, psb[:, half * 512:(half + 1) * 512], ops)
                    p.wait('scalar', 'PE')
                    if p.on('scalar'):
                        ins = p.eng.activation(out=relu1[:, oc, th * 512:(th + 1) * 512],
                                               in_=psb[:, half * 512:(half + 1) * 512],
                                               func=AF.Relu, bias=b1b[:, oc:oc + 1])
                        ins.then_inc(p.sems['A'], 1)
                    p.cnt['A'] += 1
                    g[gg] = p.cnt['A']
                    g['n'] += 1
            p.wait('vector', 'A')
            if p.on('vector'):
                for oc in range(4):
                    p.eng.tensor_reduce(out=q1[:, oc * POOL:(oc + 1) * POOL],
                                        in_=relu1[:, oc, :]
                                        .rearrange("p (n w) -> p n w", w=4),
                                        axis=AX.X, op=ALU.add)
                p.eng.drain()
                ins = p.eng.tensor_scalar(out=pool1.rearrange("p a b -> p (a b)"),
                                          in0=q1[:, 0:4 * POOL],
                                          scalar1=0.25, scalar2=None, op0=ALU.mult)
                p.eng.drain()
                ins.then_inc(p.sems['V'], 1)
            p.cnt['V'] += 1
            pe_phase_barrier()
            g = {'n': 0}
            for oc in range(8):
                gg = g['n']
                half = gg % 2
                if gg >= 2:
                    p.wait('tensor', 'V', g[gg - 2])
                ops = [(ws[:, 1, ic, oc * P:(oc + 1) * P], pool1[:, ic, :])
                       for ic in range(4)]
                mm_accum(p, psb[:, half * 512:half * 512 + POOL], ops)
                p.wait('vector', 'PE')
                if p.on('vector'):
                    ins = p.eng.tensor_scalar(out=compT[:, oc, :],
                                              in0=psb[:, half * 512:half * 512 + POOL],
                                              scalar1=b2b[:, oc:oc + 1], scalar2=None,
                                              op0=ALU.add)
                    ins.then_inc(p.sems['V'], 1)
                p.cnt['V'] += 1
                g[gg] = p.cnt['V']
                g['n'] += 1
            p.wait('sync', 'PE')
            p.dma('sync', 'SD', ws[:, 2, :, :],
                  s_att['wk'].rearrange("(ic p) o -> p ic o", p=P))
            p.dma('sync', 'SD', ws[:, 3, :, :],
                  s_att['wv'].rearrange("(ic p) o -> p ic o", p=P))
            p.wait('tensor', 'SD')
            for oc in range(8):
                gg = g['n']
                half = gg % 2
                p.wait('tensor', 'V', g[gg - 2])
                ops = [(ws[:, 2, ic, oc * P:(oc + 1) * P], compT[:, ic, :])
                       for ic in range(8)]
                mm_accum(p, psb[:, half * 512:half * 512 + POOL], ops)
                p.wait('vector', 'PE')
                if p.on('vector'):
                    ins = p.eng.tensor_copy(out=kct[:, oc, :],
                                            in_=psb[:, half * 512:half * 512 + POOL])
                    ins.then_inc(p.sems['V'], 1)
                p.cnt['V'] += 1
                g[gg] = p.cnt['V']
                g['n'] += 1
            for kc in range(2):
                for oh in range(2):
                    gg = g['n']
                    half = gg % 2
                    p.wait('tensor', 'V', g[gg - 2])
                    ops = [(compT[:, ic, kc * P:(kc + 1) * P],
                            ws[:, 3, ic, oh * 512:(oh + 1) * 512]) for ic in range(8)]
                    mm_accum(p, psb[:, half * 512:(half + 1) * 512], ops)
                    p.wait('vector', 'PE')
                    if p.on('vector'):
                        ins = p.eng.tensor_copy(out=vc[:, kc, oh * 512:(oh + 1) * 512],
                                                in_=psb[:, half * 512:(half + 1) * 512])
                        ins.then_inc(p.sems['V'], 1)
                    p.cnt['V'] += 1
                    g[gg] = p.cnt['V']
                    g['n'] += 1
            if stop == 'dbga':
                p.wait('sync', 'V')
                p.dma('sync', 'SD', out[0:128, 0:1024],
                      compT.rearrange("p a b -> p (a b)").bitcast(F32))
                p.dma('sync', 'SD', out[128:256, 0:1024],
                      kct.rearrange("p a b -> p (a b)").bitcast(F32))
                p.dma('sync', 'SD', out[256:384, 0:1024],
                      vc.rearrange("p a b -> p (a b)").bitcast(F32))
                p.dma('sync', 'SD', out[384:512, 0:512],
                      lnT1.rearrange("p a b -> p (a b)").bitcast(F32)[:, 0:512])
                p.wait('sync', 'SD')
                return
            p.wait('sync', 'PE')
            p.dma('sync', 'SD', ws[:, 0, :, :],
                  s_att['wq'].rearrange("(ic p) o -> p ic o", p=P))
            p.wait('tensor', 'SD')
            proj_fm(0, lnT1, qt1, 8, CTX, scale_const=0.125, evac='scalar')
            proj_fm(2, lnT1, kt1, 8, CTX, evac='vector')
            pe_phase_barrier()
            g = {'n': 0}
            for s8 in range(8):
                for oh in range(2):
                    gg = g['n']
                    half = gg % 2
                    if gg >= 2:
                        p.wait('tensor', 'V', g[gg - 2])
                    ops = [(lnT1[:, ic, s8 * P:(s8 + 1) * P],
                            ws[:, 3, ic, oh * 512:(oh + 1) * 512]) for ic in range(8)]
                    mm_accum(p, psb[:, half * 512:(half + 1) * 512], ops)
                    p.wait('vector', 'PE')
                    if p.on('vector'):
                        ins = p.eng.tensor_copy(out=v1[:, s8, oh * 512:(oh + 1) * 512],
                                                in_=psb[:, half * 512:(half + 1) * 512])
                        ins.then_inc(p.sems['V'], 1)
                    p.cnt['V'] += 1
                    g[gg] = p.cnt['V']
                    g['n'] += 1
            pe_phase_barrier()
            a1_rel = {0: 0, 1: 0}
            for qs in range(8):
                kend = (qs + 1) * P
                for h in range(NH):
                    par = h % 2
                    sc = psa[:, par * 1024:par * 1024 + kend]
                    p.wait('tensor', 'A', a1_rel[par])
                    po = (h % 2) * 64
                    lastmm = None
                    if p.on('tensor'):
                        nk = (kend + 511) // 512
                        for kk in range(nk):
                            k0, k1 = kk * 512, min(kend, (kk + 1) * 512)
                            lastmm = p.eng.matmul(
                                sc[:, k0:k1],
                                qt1[po:po + 64, h // 2, qs * P:(qs + 1) * P],
                                kt1[po:po + 64, h // 2, k0:k1],
                                start=True, stop=True)
                    p.inc('PE', lastmm)
                    p.wait('vector', 'PE')
                    if p.on('vector'):
                        p.eng.tensor_tensor(out=sc[:, qs * P:(qs + 1) * P],
                                            in0=sc[:, qs * P:(qs + 1) * P],
                                            in1=maskb[:], op=ALU.add)
                        p.eng.drain()
                        ins = p.eng.reduce_max(out=stc('negmax', h), in_=sc[:, 0:kend],
                                               axis=AX.X, negate=True)
                        p.eng.drain()
                        ins.then_inc(p.sems['V'], 1)
                    p.cnt['V'] += 1
                    p.wait('scalar', 'V')
                    p.wait('scalar', 'PE')
                    if p.on('scalar'):
                        ins = p.eng.activation(out=pr1[h % 2][:, 0:kend],
                                               in_=sc[:, 0:kend], func=AF.Exp,
                                               bias=stc('negmax', h),
                                               accum_out=stc('sume', h))
                        ins.then_inc(p.sems['A'], 1)
                    p.cnt['A'] += 1
                    a1_rel[par] = p.cnt['A']
                    p.wait('vector', 'A')
                    if p.on('vector'):
                        p.eng.reciprocal(out=stc('rsum', h), in_=stc('sume', h))
                        p.eng.drain()
                        ins = p.eng.tensor_scalar(out=pr1[h % 2][:, 0:kend],
                                                  in0=pr1[h % 2][:, 0:kend],
                                                  scalar1=stc('rsum', h), scalar2=None,
                                                  op0=ALU.mult)
                        p.eng.drain()
                        ins.then_inc(p.sems['V'], 1)
                    p.cnt['V'] += 1
                    p.wait('tensor', 'V')
                    pairs = [(psc[:, kc * P:(kc + 1) * P],
                              pr1[h % 2][:, kc * P:(kc + 1) * P])
                             for kc in range(qs + 1)]
                    transpose_batch(p, pairs)
                    p.wait('vector', 'PE')
                    if p.on('vector'):
                        ins = p.eng.tensor_copy(
                            out=prT1[h % 2][:, 0:qs + 1, :],
                            in_=psc[:, 0:kend].rearrange("p (a b) -> p a b", b=P))
                        ins.then_inc(p.sems['V'], 1)
                    p.cnt['V'] += 1
                    p.wait('tensor', 'V')
                    pvv = psb.rearrange("p (a b) -> p a b", b=P)
                    lastmm = None
                    if p.on('tensor'):
                        for kc in range(qs + 1):
                            lastmm = p.eng.matmul(pvv[po:po + 64, h // 2, :],
                                                  v1[:, kc, h * HD:(h + 1) * HD],
                                                  prT1[h % 2][:, kc, :],
                                                  start=(kc == 0), stop=(kc == qs))
                    p.inc('PE', lastmm)
                p.wait('vector', 'PE')
                if p.on('vector'):
                    ins = p.eng.tensor_copy(out=ot1[:, :, qs * P:(qs + 1) * P],
                                            in_=psb.rearrange("p (a b) -> p a b", b=P))
                    ins.then_inc(p.sems['V'], 1)
                p.cnt['V'] += 1
            if stop == 'dbgb':
                p.wait('sync', 'V')
                p.wait('sync', 'A')
                for nm, buf in [(0, qt1), (1, kt1), (2, v1), (3, ot1)]:
                    flat = buf.rearrange("p a b -> p (a b)").bitcast(F32)
                    for c in range(4):
                        p.dma('sync', 'SD', out[nm * 512 + c * 128:nm * 512 + (c + 1) * 128, :],
                              flat[:, c * 1024:(c + 1) * 1024])
                p.wait('sync', 'SD')
                return
            p.wait('sync', 'PE')
            p.dma('sync', 'SD', ws[:, 1, :, :],
                  s_att['wo'].rearrange("(ic p) o -> p ic o", p=P))
            p.wait('tensor', 'SD')
            pe_phase_barrier()
            g = {'n': 0}
            for s8 in range(8):
                xb = xsb[s8 // 4]
                s = s8 % 4
                for oh in range(2):
                    gg = g['n']
                    half = gg % 2
                    if gg >= 2:
                        p.wait('tensor', 'V', g[gg - 2])
                    ops = [(ot1[:, ic, s8 * P:(s8 + 1) * P],
                            ws[:, 1, ic, oh * 512:(oh + 1) * 512]) for ic in range(8)]
                    mm_accum(p, psb[:, half * 512:(half + 1) * 512], ops)
                    p.wait('vector', 'PE')
                    if p.on('vector'):
                        ins = p.eng.tensor_tensor(
                            out=xb[:, s, oh * 512:(oh + 1) * 512],
                            in0=psb[:, half * 512:(half + 1) * 512],
                            in1=xb[:, s, oh * 512:(oh + 1) * 512], op=ALU.add)
                        ins.then_inc(p.sems['V'], 1)
                    p.cnt['V'] += 1
                    g[gg] = p.cnt['V']
                    g['n'] += 1
            for sl in range(4):
                slot_rel[sl] = p.cnt['PE']

        def full_barrier():
            for ek in ('tensor', 'vector', 'scalar', 'sync', 'gpsimd'):
                for ch in ('PE', 'V', 'A'):
                    p.wait(ek, ch)

        def store_tile(t, xb):
            p.wait('gpsimd', 'V')
            p.dma('gpsimd', 'GD',
                  out[t * T:(t + 1) * T, :].rearrange("(s p) d -> p s d", p=P), xb[:])

        ffn_call(xsb[0])
        if stop in ('dbgx', 'dbgg'):
            return
        if stop == 'ffn1t0':
            store_tile(0, xsb[0])
            p.wait('gpsimd', 'GD')
            return
        if stop == 'ffn1b2':
            ffn_call(xsb[1])
            store_tile(0, xsb[0])
            store_tile(1, xsb[1])
            p.wait('gpsimd', 'GD')
            return
        ffn_call(xsb[1])
        if stop == 'ffn1both':
            store_tile(0, xsb[0])
            store_tile(1, xsb[1])
            p.wait('gpsimd', 'GD')
            return
        attn1()
        if stop in ('dbga', 'dbgb'):
            return
        if stop == 'attn1':
            store_tile(0, xsb[0])
            store_tile(1, xsb[1])
            p.wait('gpsimd', 'GD')
            return
        ffn_call(xsb[0])
        ffn_call(xsb[1])
        store_tile(0, xsb[0])
        store_tile(1, xsb[1])
        if stop == 'ffn2':
            p.wait('gpsimd', 'GD')
            return
        for t in range(2, NT):
            bi = t % 2
            p.wait('gpsimd', 'GD')
            p.dma('gpsimd', 'GD', xsb[bi][:],
                  x_in[t * T:(t + 1) * T, :].rearrange("(s p) d -> p s d", p=P))
            ffn_call(xsb[bi])
            if stop != 'tile2nf':
                attn2(xsb[bi])
            if stop in ('a2ln', 'a2q', 'a2s', 'a2pv', 'a2s0', 'a2s1', 'a2s2', 'a2s3'):
                store_tile(t, xsb[bi])
                p.wait('gpsimd', 'GD')
                return
            ffn_call(xsb[bi])
            store_tile(t, xsb[bi])
            if stop in ('tile2', 'tile2nf') and t == 2:
                p.wait('gpsimd', 'GD')
                return
        p.wait('gpsimd', 'GD')

    walks = {}

    def run_walk(ek, eng):
        w = Walk(nc, ek, eng, sems)
        program(w)
        walks[ek] = w

    block.sync(lambda eng: run_walk('sync', eng))
    block.gpsimd(lambda eng: run_walk('gpsimd', eng))
    block.tensor(lambda eng: run_walk('tensor', eng))
    block.vector(lambda eng: run_walk('vector', eng))
    block.scalar(lambda eng: run_walk('scalar', eng))

    cnts = [w.cnt for w in walks.values()]
    for c in cnts[1:]:
        assert c == cnts[0], f"walk divergence: {cnts[0]} vs {c}"

    ctx.close()
    return nc


_NC_CACHE = {}


def kernel(L0_attn_hidden_states, L0_attn_Wq, L0_attn_Wk, L0_attn_Wv, L0_attn_Wo,
           L0_attn_W1, L0_attn_b1, L0_attn_W2, L0_attn_b2,
           L0_ffn_w_gate, L0_ffn_w_up, L0_ffn_w_down):
    hs = np.ascontiguousarray(np.asarray(L0_attn_hidden_states, dtype=np.float32))
    wmap = {k: np.ascontiguousarray(np.asarray(v, dtype=np.float32)) for k, v in [
        ("wq", L0_attn_Wq), ("wk", L0_attn_Wk), ("wv", L0_attn_Wv), ("wo", L0_attn_Wo),
        ("w1", L0_attn_W1), ("b1", L0_attn_b1), ("w2", L0_attn_W2), ("b2", L0_attn_b2),
        ("wg", L0_ffn_w_gate), ("wu", L0_ffn_w_up), ("wd", L0_ffn_w_down)]}
    if 'nc' not in _NC_CACHE:
        _NC_CACHE['nc'] = build_program()
    nc = _NC_CACHE['nc']
    in_maps = [dict(wmap, x=hs[c]) for c in range(8)]
    res = run_bass_kernel_spmd(nc, in_maps, core_ids=list(range(8)))
    return np.stack([r["out"] for r in res.results], axis=0)
